# version 13
# baseline (speedup 1.0000x reference)
import math
import numpy as np
import concourse.bass as bass
import concourse.mybir as mybir
from concourse.bass_utils import run_bass_kernel_spmd

F32 = mybir.dt.float32
BF16 = mybir.dt.bfloat16
U8 = mybir.dt.uint8
AF = mybir.ActivationFunctionType
ALU = mybir.AluOpType

C = 1024
NCH = 8
T = 512
NT = 4096
NTILES = NT // T
NH = 16
HD = 64
EPS_ATTN = 1e-6
EPS_LN1 = 1e-12
EPS_LN = 1e-5
RC = 1.0 / C

ENGS = ("te", "act", "dve", "gp", "sp")
SELF_WAIT = {"act", "dve", "gp"}


class Buf:
    __slots__ = ("w", "r")

    def __init__(self):
        self.w = None
        self.r = []


class Sched:
    def __init__(self):
        self.cnt = {e: 0 for e in ENGS}
        self.dcnt = {}
        self.last_wait = {e: {} for e in ENGS}
        self.prog = {e: [] for e in ENGS}
        self.barrier_toks = []

    def _collect(self, eng, reads, writes, deps):
        need = list(deps) + list(self.barrier_toks)
        for b in reads:
            if b.w is not None:
                need.append(b.w)
        for b in writes:
            need.extend(b.r)
            if b.w is not None:
                need.append(b.w)
        waits = []
        for (sname, v) in need:
            if sname == eng and eng not in SELF_WAIT:
                continue
            lw = self.last_wait[eng].get(sname, 0)
            if v > lw:
                self.last_wait[eng][sname] = v
                waits.append((sname, v))
        return waits

    def op(self, eng, fn, reads=(), writes=(), deps=()):
        waits = self._collect(eng, reads, writes, deps)
        self.cnt[eng] += 1
        tok = (eng, self.cnt[eng])
        self.prog[eng].append((waits, fn, eng, 1))
        for b in reads:
            b.r.append(tok)
        for b in writes:
            b.w = tok
            b.r = []
        return tok

    def dma(self, qeng, dsem, out, in_, reads=(), writes=(), deps=(), **kw):
        sem = dsem
        waits = self._collect(qeng, reads, writes, deps)
        fn = lambda e, out=out, in_=in_, kw=kw: e.dma_start(out=out, in_=in_, **kw)
        self.prog[qeng].append((waits, fn, sem, 16))
        self.dcnt[sem] = self.dcnt.get(sem, 0) + 16
        tok = (sem, self.dcnt[sem])
        for b in reads:
            b.r.append(tok)
        for b in writes:
            b.w = tok
            b.r = []
        return tok

    def barrier(self):
        toks = [(e, self.cnt[e]) for e in ("te", "act", "dve", "gp") if self.cnt[e]]
        toks += [(sn, v) for sn, v in self.dcnt.items() if v]
        self.barrier_toks = toks


def build_program():
    nc = bass.Bass("TRN2", target_bir_lowering=False)

    xT = nc.dram_tensor("xT", [C, NT], F32, kind="ExternalInput").ap()
    outT = nc.dram_tensor("outT", [C, NT], F32, kind="ExternalOutput").ap()
    wnames = ["wq", "wk", "wv", "wd", "wr", "wk2", "wv2"]
    W = {n: nc.dram_tensor(n, [C, C], F32, kind="ExternalInput").ap() for n in wnames}
    econst = nc.dram_tensor("econst", [128, C], F32, kind="ExternalInput").ap()
    gconst = nc.dram_tensor("gconst", [128, C], F32, kind="ExternalInput").ap()
    oneconst = nc.dram_tensor("oneconst", [128, 128], F32, kind="ExternalInput").ap()
    vnames = ["bq", "bk", "bv", "bd", "gamma", "beta", "mur", "muk"]
    VEC = {n: nc.dram_tensor(n, [C], F32, kind="ExternalInput").ap() for n in vnames}

    s = Sched()

    from contextlib import ExitStack
    with ExitStack() as stack:
        en = stack.enter_context
        qall_t = en(nc.sbuf_tensor("qall", [128, NCH * NT], BF16))
        wslot_t = en(nc.sbuf_tensor("wslot", [128, 4 * NCH * C], BF16))
        arena_t = en(nc.sbuf_tensor("arena", [128, 63488], U8))
        G_t = en(nc.sbuf_tensor("G128", [128, 1024], F32))
        ONE_t = en(nc.sbuf_tensor("ONE128", [128, 128], F32))
        AB_t = en(nc.sbuf_tensor("AB", [128, 1024], F32))
        E_t = en(nc.sbuf_tensor("EALL", [128, 1024], F32))
        ZP_t = en(nc.sbuf_tensor("ZP", [128, 512], F32))
        kvb_t = en(nc.sbuf_tensor("kvb", [128, 1024], BF16))
        KS_t = en(nc.sbuf_tensor("KS", [128, 256], BF16))
        consts_t = en(nc.sbuf_tensor("consts", [128, 56], F32))
        carry_t = en(nc.sbuf_tensor("carry", [128, 8], F32))
        onescol_t = en(nc.sbuf_tensor("onescol", [128, 8], BF16))
        cbias_t = en(nc.sbuf_tensor("cbias", [128, 4], F32))
        pm0 = en(nc.psum_tensor("pm0", [128, 512], F32))
        pm1 = en(nc.psum_tensor("pm1", [128, 512], F32))
        pm2 = en(nc.psum_tensor("pm2", [128, 512], F32))
        pm3 = en(nc.psum_tensor("pm3", [128, 512], F32))
        pm4 = en(nc.psum_tensor("pm4", [128, 512], F32))
        pkv_t = en(nc.psum_tensor("pkv", [128, 512], F32))
        pax0 = en(nc.psum_tensor("pax0", [128, 512], F32))
        pax1 = en(nc.psum_tensor("pax1", [128, 512], F32))
        sem_te = en(nc.semaphore("s_te"))
        sem_act = en(nc.semaphore("s_act"))
        sem_dve = en(nc.semaphore("s_dve"))
        sem_gp = en(nc.semaphore("s_gp"))
        block = en(nc.Block())
        qall = qall_t.ap()
        wslot = wslot_t.ap()
        arena = arena_t.ap()
        G = G_t.ap()
        ONE = ONE_t.ap()
        AB = AB_t.ap()
        E = E_t.ap()
        ZP = ZP_t.ap()
        kvb = kvb_t.ap()
        KS = KS_t.ap()
        consts = consts_t.ap()
        carry = carry_t.ap()
        onescol = onescol_t.ap()
        pm = [p.ap() for p in (pm0, pm1, pm2, pm3, pm4)]
        pkv = pkv_t.ap()
        pax = [pax0.ap(), pax1.ap()]
        cbias = cbias_t.ap()
        for _i, _v in enumerate([EPS_ATTN, EPS_LN1, EPS_LN]):
            nc.const_aps.register(F32, _v, cbias[:, _i:_i + 1])

        def AV(off_bytes, cols, dt):
            width = cols * (2 if dt == BF16 else 4)
            return arena[:, off_bytes:off_bytes + width].bitcast(dt)

        xf2 = [AV(ch * 2048, T, F32) for ch in range(NCH)]
        ka = [AV(ch * 1024, T, BF16) for ch in range(NCH)]
        hs = [AV(16384 + ch * 2048, T, F32) for ch in range(NCH)]
        x1 = [AV(32768 + ch * 2048, T, F32) for ch in range(NCH)]
        mix = [AV(49152 + ch * 1024, T, BF16) for ch in range(NCH)]
        mstage = [AV(49152, 1024, F32), AV(53248, 1024, F32)]
        pool = [AV(57344 + i * 2048, T, F32) for i in range(2)]
        bb16 = [AV(61440 + i * 1024, T, BF16) for i in range(2)]
        xb = [[AV(p * 8192 + ch * 1024, T, BF16) for ch in range(NCH)] for p in range(2)]
        xfp = [AV(16384 + i * 2048, T, F32) for i in range(4)]
        ktm = [AV(24576, 1024, BF16), AV(26624, 1024, BF16),
               AV(53248, 1024, BF16)]
        vtm = [AV(28672, 1024, BF16), AV(30720, 1024, BF16),
               AV(55296, 1024, BF16)]
        bias_tm = AV(32768, 2048, F32)
        poolp1 = [AV(40960 + i * 2048, T, F32) for i in range(8)]
        kvacc = AV(57344, 512, F32)
        ksacc = AV(59392, 16, F32)

        a_slot = AB[0:1, 0:512]
        b_slot = AB[0:1, 512:1024]
        stA = AB[0:1, 512:1024]
        stB = pool[0].bitcast(F32)[0:1, 0:512]
        stC = AB[0:1, 0:512]
        zscr = AV(61440, 512, F32)[0:16, 0:512]
        z16 = ZP[0:16, 0:512]

        cn = ["bq", "bd", "mur", "muk", "omr", "omk", "beta"]
        ccol = {n: consts[:, 8 * i:8 * i + 8] for i, n in enumerate(cn)}

        B_q = [[Buf() for _ in range(NCH)] for _ in range(NTILES)]
        B_w = {n: [Buf() for _ in range(NCH)] for n in wnames}
        B_xf2 = [Buf() for _ in range(NCH)]
        B_hs = [Buf() for _ in range(NCH)]
        B_x1 = [Buf() for _ in range(NCH)]
        B_ka = [Buf() for _ in range(NCH)]
        B_mix = [Buf() for _ in range(NCH)]
        B_pool = [Buf() for _ in range(2)]
        B_bb16 = [Buf() for _ in range(2)]
        B_xb = [[Buf() for _ in range(NCH)] for _ in range(2)]
        B_xfp = [Buf() for _ in range(4)]
        B_ktm = [Buf() for _ in range(3)]
        B_vtm = [Buf() for _ in range(3)]

        B_biastm = Buf()
        B_poolp1 = [Buf() for _ in range(8)]
        B_poolp1[6] = B_ktm[2]
        B_poolp1[7] = B_vtm[2]
        B_pm = [Buf() for _ in range(5)]
        B_pkv = Buf()
        B_pax = [Buf() for _ in range(2)]
        B_ksum = [Buf() for _ in range(2)]
        B_kvb = [Buf() for _ in range(2)]
        B_KS = [Buf() for _ in range(2)]
        B_carry = Buf()
        B_G = Buf()
        B_ONE = Buf()
        B_E = Buf()
        B_ZP = Buf()
        B_a = Buf()
        B_b = Buf()
        B_stA = Buf()
        B_stB = Buf()
        B_stC = Buf()
        B_zscr = Buf()
        B_consts = Buf()
        B_onescol = Buf()
        B_cbias = Buf()
        B_out = [Buf() for _ in range(NCH)]

        pm_i = [0]
        pm_depth = [5]

        def next_pm():
            i = pm_i[0] % pm_depth[0]
            pm_i[0] += 1
            return i

        pax_i = [0]
        pax_depth = [2]

        def next_pax():
            if pax_depth[0] == 8:
                return next_pm()
            i = pax_i[0] % pax_depth[0]
            pax_i[0] += 1
            return i

        def ka_access(ch):
            return [B_ka[ch], B_xf2[ch // 2]]

        for i, n in enumerate(["bq", "bd", "mur", "muk"]):
            s.dma("sp", "dset", ccol[n], VEC[n].rearrange("(j p) -> p j", p=128),
                  writes=[B_consts], allow_slow_non_contiguous=True)
        s.dma("sp", "dset", ccol["beta"], VEC["beta"].rearrange("(j p) -> p j", p=128),
              writes=[B_consts], allow_slow_non_contiguous=True)
        s.dma("sp", "dset", E[:], econst, writes=[B_E])
        s.dma("sp", "dset", G[:], gconst, writes=[B_G])
        s.dma("sp", "dset", ONE[:], oneconst, writes=[B_ONE])
        s.dma("sp", "dset", bias_tm[:, 0:1024],
              VEC["bk"][None, :].partition_broadcast(128), writes=[B_biastm])
        s.dma("sp", "dset", bias_tm[:, 1024:2048],
              VEC["bv"][None, :].partition_broadcast(128), writes=[B_biastm])
        for _b in (B_consts, B_biastm, B_E, B_G, B_ONE):
            _b.w = ("dset", s.dcnt["dset"])
        s.op("dve", lambda e: e.tensor_scalar(out=ccol["omr"], in0=ccol["mur"],
             scalar1=-1.0, scalar2=1.0, op0=ALU.mult, op1=ALU.add),
             reads=[B_consts], writes=[B_consts])
        s.op("dve", lambda e: e.tensor_scalar(out=ccol["omk"], in0=ccol["muk"],
             scalar1=-1.0, scalar2=1.0, op0=ALU.mult, op1=ALU.add),
             reads=[B_consts], writes=[B_consts])
        s.op("dve", lambda e: e.memset(AB[:], 0.0),
             writes=[B_a, B_b, B_stA, B_stB, B_stC, B_zscr])
        s.op("dve", lambda e: e.memset(ZP[:], 0.0), writes=[B_ZP])
        s.op("dve", lambda e: e.memset(KS[:], 0.0), writes=[B_KS[0], B_KS[1]])
        s.op("dve", lambda e: e.memset(onescol[:, 0:1], 1.0), writes=[B_onescol])
        for _i, _v in enumerate([EPS_ATTN, EPS_LN1, EPS_LN]):
            s.op("dve", lambda e, _i=_i, _v=_v: e.memset(cbias[:, _i:_i + 1], _v),
                 writes=[B_cbias])

        SLOT = {"wq": 0, "wk": 1, "wv": 2, "wd": 3, "wr": 0, "wk2": 1, "wv2": 2}

        def wsl(n, ch):
            return wslot[:, (SLOT[n] * NCH + ch) * C:(SLOT[n] * NCH + ch + 1) * C]

        def load_weight(n, stage_aps, stage_buf_sets, qeng, sems):
            for ch in range(NCH):
                sa = stage_aps[ch % 2]
                sbs = stage_buf_sets[ch % 2]
                s.dma(qeng, sems[ch % 2], sa, W[n][128 * ch:128 * (ch + 1), :],
                      writes=sbs)
                s.op("dve", lambda e, sa=sa, n=n, ch=ch: e.tensor_scalar(
                    out=wsl(n, ch), in0=sa, scalar1=1.0, scalar2=None, op0=ALU.mult),
                    reads=sbs, writes=[B_w[n][ch]])

        wstage = [AV(40960, 1024, F32), AV(45056, 1024, F32)]
        for n in ["wq", "wk", "wv", "wd"]:
            load_weight(n, wstage,
                        [[B_poolp1[0], B_poolp1[1]], [B_poolp1[2], B_poolp1[3]]],
                        "sp", ("dw0", "dw1"))

        for i in range(NTILES):
            par = i % 2
            batch = i // (NTILES // 2)
            t0 = i * T
            first_of_batch = (i % (NTILES // 2) == 0)
            last_of_batch = (i % (NTILES // 2) == (NTILES // 2) - 1)
            for ch in range(NCH):
                sl = xfp[ch % 4]
                s.dma("sp", "dx%d" % (ch % 4), sl,
                      xT[128 * ch:128 * (ch + 1), t0:t0 + T],
                      writes=[B_xfp[ch % 4]])
                s.op("dve", lambda e, sl=sl, par=par, ch=ch: e.tensor_scalar(
                    out=xb[par][ch], in0=sl, scalar1=1.0, scalar2=None, op0=ALU.mult),
                    reads=[B_xfp[ch % 4]], writes=[B_xb[par][ch]])
            for dch in range(NCH):
                b = next_pm()
                for cch in range(NCH):
                    s.op("te", lambda e, b=b, dch=dch, cch=cch, par=par: e.matmul(
                        pm[b][:, 0:T], wsl("wq", cch)[:, 128 * dch:128 * (dch + 1)],
                        xb[par][cch], start=(cch == 0), stop=(cch == NCH - 1),
                        skip_group_check=True),
                        reads=[B_w["wq"][cch], B_xb[par][cch]], writes=[B_pm[b]])
                ie, ir = (dch % 3) * 2, (dch % 3) * 2 + 1
                pe, pr = poolp1[ie], poolp1[ir]
                bq_ap = ccol["bq"][:, dch:dch + 1]
                s.op("act", lambda e, b=b, pe=pe, bq_ap=bq_ap: e.activation(
                    out=pe, in_=pm[b][:, 0:T], func=AF.Exp, bias=bq_ap, scale=1.0),
                    reads=[B_pm[b], B_consts], writes=[B_poolp1[ie]])
                s.op("act", lambda e, b=b, pr=pr, bq_ap=bq_ap: e.activation(
                    out=pr, in_=pm[b][:, 0:T], func=AF.Relu, bias=bq_ap, scale=1.0),
                    reads=[B_pm[b], B_consts], writes=[B_poolp1[ir]])
                qdst = qall[:, dch * NT + t0:dch * NT + t0 + T]
                s.op("dve", lambda e, pe=pe, pr=pr, qdst=qdst: e.scalar_tensor_tensor(
                    out=qdst, in0=pe, scalar=1.0, in1=pr, op0=ALU.min, op1=ALU.add),
                    reads=[B_poolp1[ie], B_poolp1[ir]], writes=[B_q[i][dch]])
            for sub in range(T // 128):
                p2 = (i * (T // 128) + sub) % 3
                first_sub = first_of_batch and sub == 0
                last_sub = last_of_batch and sub == (T // 128) - 1
                for dh in range(2):
                    bk_ = next_pm()
                    bv_ = next_pm()
                    for cch in range(NCH):
                        lhs = xb[par][cch][:, 128 * sub:128 * (sub + 1)]
                        s.op("te", lambda e, bk_=bk_, lhs=lhs, cch=cch, dh=dh: e.matmul(
                            pm[bk_][:, 0:T], lhs,
                            wsl("wk", cch)[:, 512 * dh:512 * (dh + 1)],
                            start=(cch == 0), stop=(cch == NCH - 1),
                            skip_group_check=True),
                            reads=[B_xb[par][cch], B_w["wk"][cch]], writes=[B_pm[bk_]])
                    for cch in range(NCH):
                        lhs = xb[par][cch][:, 128 * sub:128 * (sub + 1)]
                        s.op("te", lambda e, bv_=bv_, lhs=lhs, cch=cch, dh=dh: e.matmul(
                            pm[bv_][:, 0:T], lhs,
                            wsl("wv", cch)[:, 512 * dh:512 * (dh + 1)],
                            start=(cch == 0), stop=(cch == NCH - 1),
                            skip_group_check=True),
                            reads=[B_xb[par][cch], B_w["wv"][cch]], writes=[B_pm[bv_]])
                    iu, ie, ir = 3 * dh, 3 * dh + 1, 3 * dh + 2
                    pu, pe, pr = poolp1[iu], poolp1[ie], poolp1[ir]
                    bsl = bias_tm[:, 512 * dh:512 * (dh + 1)]
                    s.op("dve", lambda e, bk_=bk_, pu=pu, bsl=bsl: e.tensor_tensor(
                        out=pu, in0=pm[bk_][:, 0:T], in1=bsl, op=ALU.add),
                        reads=[B_pm[bk_], B_biastm], writes=[B_poolp1[iu]])
                    s.op("act", lambda e, pu=pu, pe=pe: e.activation(
                        out=pe, in_=pu, func=AF.Exp),
                        reads=[B_poolp1[iu]], writes=[B_poolp1[ie]])
                    s.op("act", lambda e, pu=pu, pr=pr: e.activation(
                        out=pr, in_=pu, func=AF.Relu),
                        reads=[B_poolp1[iu]], writes=[B_poolp1[ir]])
                    kdst = ktm[p2][:, 512 * dh:512 * (dh + 1)]
                    s.op("dve", lambda e, pe=pe, pr=pr, kdst=kdst:
                         e.scalar_tensor_tensor(
                        out=kdst, in0=pe, scalar=1.0, in1=pr, op0=ALU.min,
                        op1=ALU.add),
                        reads=[B_poolp1[ie], B_poolp1[ir]], writes=[B_ktm[p2]])
                    vdst = vtm[p2][:, 512 * dh:512 * (dh + 1)]
                    bsv = bias_tm[:, 1024 + 512 * dh:1024 + 512 * (dh + 1)]
                    s.op("dve", lambda e, bv_=bv_, vdst=vdst, bsv=bsv: e.tensor_tensor(
                        out=vdst, in0=pm[bv_][:, 0:T], in1=bsv, op=ALU.add),
                        reads=[B_pm[bv_], B_biastm], writes=[B_vtm[p2]])
                skv = (i * (T // 128) + sub) % 2
                kvb_ps = pkv if skv == 0 else pax[1]
                kvb_buf = B_pkv if skv == 0 else B_pax[1]
                kscols = 0 if skv == 0 else 8
                for ch in range(NCH):
                    for hh in range(2):
                        lhs = ktm[p2][:, ch * 128 + hh * 64:ch * 128 + hh * 64 + 64]
                        rhs = vtm[p2][:, ch * 128 + hh * 64:ch * 128 + hh * 64 + 64]
                        dst = kvb_ps[hh * 64:(hh + 1) * 64, ch * 64:(ch + 1) * 64]
                        tp = (0, 64) if hh == 1 else None
                        s.op("te", lambda e, lhs=lhs, rhs=rhs, dst=dst, tp=tp:
                             e.matmul(dst, lhs, rhs, start=True, stop=True,
                                      tile_position=tp, skip_group_check=True),
                            reads=[B_ktm[p2], B_vtm[p2]], writes=[kvb_buf])
                    s.op("te", lambda e, ch=ch, p2=p2, kscols=kscols: e.matmul(
                        pax[0][:, kscols + ch:kscols + ch + 1],
                        ktm[p2][:, ch * 128:(ch + 1) * 128], onescol[:, 0:1],
                        start=(ch == 0), stop=(ch == NCH - 1),
                        skip_group_check=True),
                        reads=[B_ktm[p2], B_onescol], writes=[B_ksum[skv]])
                if first_sub:
                    s.op("dve", lambda e, kvb_ps=kvb_ps: e.tensor_copy(
                         kvacc, kvb_ps[:, 0:512]),
                         reads=[kvb_buf], writes=[B_pool[0]])
                    s.op("dve", lambda e, kscols=kscols: e.tensor_copy(
                         ksacc[:, 0:8], pax[0][:, kscols:kscols + 8]),
                         reads=[B_ksum[skv]], writes=[B_pool[1]])
                else:
                    s.op("dve", lambda e, kvb_ps=kvb_ps: e.tensor_tensor(
                         out=kvacc, in0=kvacc,
                         in1=kvb_ps[:, 0:512], op=ALU.add),
                         reads=[kvb_buf, B_pool[0]], writes=[B_pool[0]])
                    s.op("dve", lambda e, kscols=kscols: e.tensor_tensor(
                         out=ksacc[:, 0:8],
                         in0=ksacc[:, 0:8], in1=pax[0][:, kscols:kscols + 8],
                         op=ALU.add),
                         reads=[B_ksum[skv]], writes=[B_pool[1]])
            if last_of_batch:
                s.op("dve", lambda e, batch=batch: e.tensor_scalar(
                    out=kvb[:, batch * 512:(batch + 1) * 512], in0=kvacc,
                    scalar1=1.0, scalar2=None, op0=ALU.mult),
                    reads=[B_pool[0]], writes=[B_kvb[batch]])
                for ch in range(NCH):
                    for hh in range(2):
                        h = 2 * ch + hh
                        ksrc = ksacc[hh * 64:(hh + 1) * 64, ch:ch + 1]
                        dst = KS[hh * 64:(hh + 1) * 64,
                                 batch * 128 + ch * 16 + h:
                                 batch * 128 + ch * 16 + h + 1]
                        s.op("act", lambda e, ksrc=ksrc, dst=dst: e.activation(
                            out=dst, in_=ksrc, func=AF.Copy, bias=EPS_ATTN,
                            scale=1.0),
                            reads=[B_pool[1], B_cbias], writes=[B_KS[batch]])

        s.barrier()
        pax.append(pkv)
        B_pax.append(B_pkv)
        pax_depth[0] = 3

        mst_bufs = [[B_mix[0], B_mix[1], B_mix[2], B_mix[3]],
                    [B_mix[4], B_mix[5], B_mix[6], B_mix[7]]]

        def emit_morph():
            for n in ["wr", "wk2", "wv2"]:
                load_weight(n, mstage, mst_bufs, "sp", ("dm0", "dm1"))

        def pe_keepalive(n):
            for _ in range(n):
                s.op("te", lambda e: e.ldweights(KS[:, 0:128]), reads=[B_KS[0]])

        def ln_chain(bs, bq_, eps):
            pe_keepalive(24)
            s.op("act", lambda e, bs=bs: e.activation(out=stA, in_=pm[bs][0:1, 0:T],
                 func=AF.Square, scale=RC),
                 reads=[B_pm[bs]], writes=[B_b])
            s.op("dve", lambda e, bq_=bq_: e.scalar_tensor_tensor(
                 out=stB, in0=pm[bq_][0:1, 0:T],
                 scalar=RC, in1=stA, op0=ALU.mult, op1=ALU.subtract),
                 reads=[B_pm[bq_], B_b], writes=[B_pool[0]])
            s.op("act", lambda e, eps=eps: e.activation(out=stC, in_=stB,
                 func=AF.Ln, bias=eps, scale=1.0),
                 reads=[B_pool[0], B_cbias], writes=[B_a])
            s.op("act", lambda e: e.activation(out=a_slot, in_=stC,
                 func=AF.Exp, scale=-0.5),
                 reads=[B_a], writes=[B_a])
            s.op("dve", lambda e, bs=bs: e.scalar_tensor_tensor(
                 out=b_slot, in0=pm[bs][0:1, 0:T],
                 scalar=-RC, in1=a_slot, op0=ALU.mult, op1=ALU.mult),
                 reads=[B_pm[bs], B_a], writes=[B_b])

        def stat_sum_pre(a, abuf, ch):
            s.op("act", lambda e, a=a, o=mix[ch]: e.activation(
                out=o, in_=a, func=AF.Copy), reads=[abuf], writes=[B_mix[ch]])

        def ln_stats(src_aps, src_bufs, prestaged):
            bs = next_pm()
            bq_ = next_pm()
            if not prestaged:
                for ch in range(NCH):
                    stat_sum_pre(src_aps[ch], src_bufs[ch], ch)
            for ch in range(NCH):
                s.op("te", lambda e, bs=bs, cs=mix[ch], ch=ch: e.matmul(
                    pm[bs][0:1, 0:T], onescol[:, 0:1], cs,
                    start=(ch == 0), stop=(ch == NCH - 1), skip_group_check=True),
                    reads=[B_mix[ch], B_onescol], writes=[B_pm[bs]])
                s.op("act", lambda e, cq=mix[ch], a=src_aps[ch]: e.activation(
                    out=cq, in_=a, func=AF.Square),
                    reads=[src_bufs[ch], B_mix[ch]], writes=[B_mix[ch]])
            for ch in range(NCH):
                s.op("te", lambda e, bq_=bq_, cq=mix[ch], ch=ch: e.matmul(
                    pm[bq_][0:1, 0:T], onescol[:, 0:1], cq,
                    start=(ch == 0), stop=(ch == NCH - 1), skip_group_check=True),
                    reads=[B_mix[ch], B_onescol], writes=[B_pm[bq_]])
            return bs, bq_

        def xf2_loads(i):
            t0 = i * T
            for ch in range(NCH):
                wl = [B_xf2[ch]]
                if ch < 4:
                    wl += [B_ka[2 * ch], B_ka[2 * ch + 1]]
                s.dma("sp", "df%d" % ch, xf2[ch],
                      xT[128 * ch:128 * (ch + 1), t0:t0 + T], writes=wl)

        def attn_early(i):
            batch = i // (NTILES // 2)
            t0 = i * T
            bd_ = next_pax()
            for ch in range(NCH):
                lhs = KS[:, batch * 128 + ch * 16:batch * 128 + (ch + 1) * 16]
                rhs = qall[:, ch * NT + t0:ch * NT + t0 + T]
                s.op("te", lambda e, bd_=bd_, lhs=lhs, rhs=rhs, ch=ch: e.matmul(
                    pax[bd_][0:16, 0:T], lhs, rhs, start=(ch == 0),
                    stop=(ch == NCH - 1), skip_group_check=True),
                    reads=[B_KS[batch], B_q[i][ch]], writes=[B_pax[bd_]])
            pe_keepalive(12)
            s.op("act", lambda e, bd_=bd_: e.activation(out=z16,
                 in_=pax[bd_][0:16, 0:T], func=AF.Ln),
                 reads=[B_pax[bd_]], writes=[B_ZP])
            s.op("act", lambda e: e.activation(out=z16, in_=z16, func=AF.Exp,
                 scale=-1.0),
                 reads=[B_ZP], writes=[B_ZP])
            s.op("dve", lambda e, bd_=bd_: e.tensor_tensor(out=zscr, in0=z16,
                 in1=pax[bd_][0:16, 0:T], op=ALU.mult),
                 reads=[B_ZP, B_pax[bd_]], writes=[B_bb16[0], B_bb16[1]])
            s.op("dve", lambda e: e.tensor_scalar(out=zscr, in0=zscr, scalar1=-1.0,
                 scalar2=2.0, op0=ALU.mult, op1=ALU.add),
                 reads=[B_bb16[0]], writes=[B_bb16[0], B_bb16[1]])
            s.op("dve", lambda e: e.tensor_tensor(out=z16, in0=z16, in1=zscr,
                 op=ALU.mult),
                 reads=[B_bb16[0], B_ZP], writes=[B_ZP])
            zbanks = {}

            def emit_zb(ch):
                bz = next_pax()
                zbanks[ch] = bz
                s.op("te", lambda e, bz=bz, ch=ch: e.matmul(
                    pax[bz][:, 0:T], E[:, ch * 128:(ch + 1) * 128], ZP[:, 0:T],
                    start=True, stop=True, skip_group_check=True),
                    reads=[B_E, B_ZP], writes=[B_pax[bz]])

            emit_zb(0)
            emit_zb(1)
            for ch in range(NCH):
                if ch + 2 < NCH:
                    emit_zb(ch + 2)
                bz = zbanks[ch]
                qsl = qall[:, ch * NT + t0:ch * NT + t0 + T]
                s.op("dve", lambda e, bz=bz, ch=ch, qsl=qsl: e.tensor_tensor(
                    out=bb16[ch % 2], in0=qsl, in1=pax[bz][:, 0:T], op=ALU.mult),
                    reads=[B_q[i][ch], B_pax[bz]], writes=[B_bb16[ch % 2]])
                bc = next_pm()
                for hh in range(2):
                    lhs = kvb[hh * 64:(hh + 1) * 64,
                              batch * 512 + ch * 64:batch * 512 + (ch + 1) * 64]
                    rhs = bb16[ch % 2][hh * 64:(hh + 1) * 64, 0:T]
                    dst = pm[bc][hh * 64:(hh + 1) * 64, 0:T]
                    tp = (64, 64) if hh == 1 else None
                    s.op("te", lambda e, lhs=lhs, rhs=rhs, dst=dst, tp=tp: e.matmul(
                        dst, lhs, rhs, start=True, stop=True, tile_position=tp,
                        skip_group_check=True),
                        reads=[B_kvb[batch], B_bb16[ch % 2]], writes=[B_pm[bc]])
                s.op("act", lambda e, bc=bc, qsl=qsl: e.activation(
                    out=qsl, in_=pm[bc][:, 0:T], func=AF.Copy),
                    reads=[B_pm[bc]], writes=[B_q[i][ch]])

        def wd_block(i):
            t0 = i * T
            for dch in range(NCH):
                b = next_pm()
                for cch in range(NCH):
                    rhs = qall[:, cch * NT + t0:cch * NT + t0 + T]
                    s.op("te", lambda e, b=b, dch=dch, cch=cch, rhs=rhs: e.matmul(
                        pm[b][:, 0:T], wsl("wd", cch)[:, 128 * dch:128 * (dch + 1)],
                        rhs, start=(cch == 0), stop=(cch == NCH - 1),
                        skip_group_check=True),
                        reads=[B_w["wd"][cch], B_q[i][cch]], writes=[B_pm[b]])
                s.op("dve", lambda e, b=b, dch=dch: e.scalar_tensor_tensor(
                    out=hs[dch], in0=pm[b][:, 0:T], scalar=ccol["bd"][:, dch:dch + 1],
                    in1=xf2[dch], op0=ALU.add, op1=ALU.add),
                    reads=[B_pm[b], B_xf2[dch], B_consts], writes=[B_hs[dch]])
                stat_sum_pre(hs[dch], B_hs[dch], dch)

        def ln1_block(i):
            bs, bq_ = ln_stats(hs, B_hs, True)
            ln_chain(bs, bq_, EPS_LN1)
            for ch in range(NCH):
                ba = next_pax()
                bb = next_pax()
                gl = G[:, ch * 128:(ch + 1) * 128]
                s.op("te", lambda e, ba=ba, gl=gl: e.matmul(
                    pax[ba][:, 0:T], gl, AB[:, 0:512], start=True, stop=True,
                    skip_group_check=True),
                    reads=[B_G, B_a], writes=[B_pax[ba]])
                s.op("te", lambda e, bb=bb, gl=gl: e.matmul(
                    pax[bb][:, 0:T], gl, AB[:, 512:1024], start=True, stop=True,
                    skip_group_check=True),
                    reads=[B_G, B_b], writes=[B_pax[bb]])
                s.op("dve", lambda e, ba=ba, ch=ch: e.tensor_tensor(
                    out=hs[ch], in0=hs[ch], in1=pax[ba][:, 0:T], op=ALU.mult),
                    reads=[B_hs[ch], B_pax[ba]], writes=[B_hs[ch]])
                s.op("dve", lambda e, bb=bb, ch=ch: e.scalar_tensor_tensor(
                    out=xf2[ch], in0=pax[bb][:, 0:T],
                    scalar=ccol["beta"][:, ch:ch + 1], in1=xf2[ch],
                    op0=ALU.add, op1=ALU.add),
                    reads=[B_pax[bb], B_xf2[ch], B_consts], writes=[B_xf2[ch]])
                s.op("gp", lambda e, ch=ch: e.tensor_tensor(
                    out=hs[ch], in0=hs[ch], in1=xf2[ch], op=ALU.add),
                    reads=[B_hs[ch], B_xf2[ch]], writes=[B_hs[ch]])
                stat_sum_pre(hs[ch], B_hs[ch], ch)

        def ln2_block(i):
            bs, bq_ = ln_stats(hs, B_hs, True)
            ln_chain(bs, bq_, EPS_LN)
            ba = next_pax()
            bb = next_pax()
            s.op("te", lambda e, ba=ba: e.matmul(pax[ba][:, 0:T], ONE[:, 0:128],
                 AB[:, 0:512], start=True, stop=True, skip_group_check=True),
                 reads=[B_ONE, B_a], writes=[B_pax[ba]])
            s.op("te", lambda e, bb=bb: e.matmul(pax[bb][:, 0:T], ONE[:, 0:128],
                 AB[:, 512:1024], start=True, stop=True, skip_group_check=True),
                 reads=[B_ONE, B_b], writes=[B_pax[bb]])
            for ch in range(NCH):
                s.op("dve", lambda e, ba=ba, ch=ch: e.tensor_tensor(
                    out=x1[ch], in0=hs[ch], in1=pax[ba][:, 0:T], op=ALU.mult),
                    reads=[B_hs[ch], B_pax[ba]], writes=[B_x1[ch]])
                s.op("dve", lambda e, bb=bb, ch=ch: e.tensor_tensor(
                    out=x1[ch], in0=x1[ch], in1=pax[bb][:, 0:T], op=ALU.add),
                    reads=[B_x1[ch], B_pax[bb]], writes=[B_x1[ch]])

        def mk_mix(mu_name, om_name):
            for ch in range(NCH):
                po = pool[ch % 2]
                om = ccol[om_name][:, ch:ch + 1]
                s.op("act", lambda e, ch=ch, po=po, om=om: e.activation(
                    out=po[:, 1:T], in_=x1[ch][:, 0:T - 1], func=AF.Copy,
                    scale=om),
                    reads=[B_x1[ch], B_consts], writes=[B_pool[ch % 2]])
                s.op("act", lambda e, ch=ch, po=po, om=om: e.activation(
                    out=po[:, 0:1], in_=carry[:, ch:ch + 1], func=AF.Copy,
                    scale=om),
                    reads=[B_carry, B_consts], writes=[B_pool[ch % 2]])
                mu = ccol[mu_name][:, ch:ch + 1]
                s.op("dve", lambda e, ch=ch, po=po, mu=mu: e.scalar_tensor_tensor(
                    out=mix[ch], in0=x1[ch], scalar=mu, in1=po,
                    op0=ALU.mult, op1=ALU.add),
                    reads=[B_x1[ch], B_pool[ch % 2], B_consts],
                    writes=[B_mix[ch]])

        def ffn1(i):
            if i % (NTILES // 2) == 0:
                s.op("dve", lambda e: e.memset(carry[:], 0.0), writes=[B_carry])
            mk_mix("muk", "omk")
            for dch in range(NCH):
                b = next_pm()
                for cch in range(NCH):
                    s.op("te", lambda e, b=b, dch=dch, cch=cch: e.matmul(
                        pm[b][:, 0:T], wsl("wk2", cch)[:, 128 * dch:128 * (dch + 1)],
                        mix[cch], start=(cch == 0), stop=(cch == NCH - 1),
                        skip_group_check=True),
                        reads=[B_w["wk2"][cch], B_mix[cch]], writes=[B_pm[b]])
                po = pool[dch % 2]
                s.op("dve", lambda e, b=b, po=po: e.tensor_scalar(
                    out=po, in0=pm[b][:, 0:T], scalar1=0.0, scalar2=None,
                    op0=ALU.max),
                    reads=[B_pm[b]], writes=[B_pool[dch % 2]])
                s.op("act", lambda e, po=po, dch=dch: e.activation(
                    out=ka[dch], in_=po, func=AF.Square),
                    reads=[B_pool[dch % 2]], writes=ka_access(dch))

        def ffn2(i):
            mk_mix("mur", "omr")
            for ch in range(NCH):
                s.op("dve", lambda e, ch=ch: e.tensor_copy(
                    carry[:, ch:ch + 1], x1[ch][:, T - 1:T]),
                    reads=[B_x1[ch]], writes=[B_carry])
            for dch in range(NCH):
                br_ = next_pm()
                for cch in range(NCH):
                    s.op("te", lambda e, br_=br_, dch=dch, cch=cch: e.matmul(
                        pm[br_][:, 0:T], wsl("wr", cch)[:, 128 * dch:128 * (dch + 1)],
                        mix[cch], start=(cch == 0), stop=(cch == NCH - 1),
                        skip_group_check=True),
                        reads=[B_w["wr"][cch], B_mix[cch]], writes=[B_pm[br_]])
                bv_ = next_pm()
                for cch in range(NCH):
                    s.op("te", lambda e, bv_=bv_, dch=dch, cch=cch: e.matmul(
                        pm[bv_][:, 0:T], wsl("wv2", cch)[:, 128 * dch:128 * (dch + 1)],
                        ka[cch], start=(cch == 0), stop=(cch == NCH - 1),
                        skip_group_check=True),
                        reads=[B_w["wv2"][cch], B_ka[cch], B_xf2[cch // 2]],
                        writes=[B_pm[bv_]])
                pth = pool[dch % 2]
                prr = pool[(dch + 1) % 2]
                s.op("act", lambda e, br_=br_, pth=pth: e.activation(
                    out=pth, in_=pm[br_][:, 0:T], func=AF.Tanh, scale=0.5),
                    reads=[B_pm[br_]], writes=[B_pool[dch % 2]])
                s.op("dve", lambda e, pth=pth, prr=prr: e.tensor_scalar(
                    out=prr, in0=pth, scalar1=0.5, scalar2=0.5, op0=ALU.mult,
                    op1=ALU.add),
                    reads=[B_pool[dch % 2]], writes=[B_pool[(dch + 1) % 2]])
                s.op("dve", lambda e, bv_=bv_, prr=prr, pth=pth: e.tensor_tensor(
                    out=pth, in0=pm[bv_][:, 0:T], in1=prr, op=ALU.mult),
                    reads=[B_pm[bv_], B_pool[(dch + 1) % 2]],
                    writes=[B_pool[dch % 2]])
                s.op("gp", lambda e, pth=pth, dch=dch: e.tensor_tensor(
                    out=hs[dch], in0=pth, in1=x1[dch], op=ALU.add),
                    reads=[B_pool[dch % 2], B_x1[dch]], writes=[B_hs[dch]])

        def ln3_out(i):
            t0 = i * T
            bs, bq_ = ln_stats(hs, B_hs, False)
            ln_chain(bs, bq_, EPS_LN)
            ba = next_pax()
            bb = next_pax()
            s.op("te", lambda e, ba=ba: e.matmul(pax[ba][:, 0:T], ONE[:, 0:128],
                 AB[:, 0:512], start=True, stop=True, skip_group_check=True),
                 reads=[B_ONE, B_a], writes=[B_pax[ba]])
            s.op("te", lambda e, bb=bb: e.matmul(pax[bb][:, 0:T], ONE[:, 0:128],
                 AB[:, 512:1024], start=True, stop=True, skip_group_check=True),
                 reads=[B_ONE, B_b], writes=[B_pax[bb]])
            for ch in range(NCH):
                s.op("dve", lambda e, ba=ba, ch=ch: e.tensor_tensor(
                    out=x1[ch], in0=hs[ch], in1=pax[ba][:, 0:T], op=ALU.mult),
                    reads=[B_hs[ch], B_pax[ba]], writes=[B_x1[ch]])
                s.op("dve", lambda e, bb=bb, ch=ch: e.tensor_tensor(
                    out=x1[ch], in0=x1[ch], in1=pax[bb][:, 0:T], op=ALU.add),
                    reads=[B_x1[ch], B_pax[bb]], writes=[B_x1[ch]])
                s.dma("sp", "do%d" % ch, outT[128 * ch:128 * (ch + 1), t0:t0 + T],
                      x1[ch], reads=[B_x1[ch]], writes=[B_out[ch]])

        xf2_loads(0)
        emit_morph()
        attn_early(0)
        wd_block(0)
        for i in range(NTILES):
            ln1_block(i)
            if i + 1 < NTILES:
                attn_early(i + 1)
            ln2_block(i)
            ffn1(i)
            ffn2(i)
            ln3_out(i)
            if i + 1 < NTILES:
                xf2_loads(i + 1)
                wd_block(i + 1)

        sems = {"te": sem_te, "act": sem_act, "dve": sem_dve, "gp": sem_gp}
        for dname in sorted(s.dcnt):
            sems[dname] = en(nc.semaphore(dname))
        out_finals = [("do%d" % ch, s.dcnt["do%d" % ch]) for ch in range(NCH)]

        def play(eng_name, eng):
            for (waits, fn, inc_sem, inc_amt) in s.prog[eng_name]:
                for (sname, v) in waits:
                    eng.wait_ge(sems[sname], v)
                fn(eng).then_inc(sems[inc_sem], inc_amt)

        @block.tensor
        def _(e):
            play("te", e)

        @block.scalar
        def _(e):
            play("act", e)

        @block.vector
        def _(e):
            play("dve", e)

        @block.sync
        def _(e):
            play("sp", e)

        @block.gpsimd
        def _(e):
            play("gp", e)
            for dname, v in out_finals:
                e.wait_ge(sems[dname], v)

    return nc


_NC_CACHE = []


def kernel(**inputs):
    x = np.asarray(inputs["L0_attn_input_tensor"], dtype=np.float32)
    B, S, H = x.shape
    n_cores = 8
    per = B // n_cores

    wmap = {"wq": "L0_attn_Wq", "wk": "L0_attn_Wk", "wv": "L0_attn_Wv",
            "wd": "L0_attn_Wd", "wr": "L0_ffn_W_r", "wk2": "L0_ffn_W_k",
            "wv2": "L0_ffn_W_v"}
    vmap = {"bq": "L0_attn_bq", "bk": "L0_attn_bk", "bv": "L0_attn_bv",
            "bd": "L0_attn_bd", "gamma": "L0_attn_gamma", "beta": "L0_attn_beta",
            "mur": "L0_ffn_mu_r", "muk": "L0_ffn_mu_k"}
    common = {}
    for k, src in wmap.items():
        common[k] = np.ascontiguousarray(np.asarray(inputs[src], dtype=np.float32).T)
    for k, src in vmap.items():
        common[k] = np.ascontiguousarray(np.asarray(inputs[src], dtype=np.float32))

    ecst = np.zeros((128, H), dtype=np.float32)
    for ch in range(8):
        ecst[2 * ch, ch * 128:ch * 128 + 64] = 1.0
        ecst[2 * ch + 1, ch * 128 + 64:ch * 128 + 128] = 1.0
    gcst = np.zeros((128, H), dtype=np.float32)
    gcst[0, :] = common["gamma"]
    ocst = np.zeros((128, 128), dtype=np.float32)
    ocst[0, :] = 1.0
    common["econst"] = ecst
    common["gconst"] = gcst
    common["oneconst"] = ocst

    in_maps = []
    for c in range(n_cores):
        shard = x[c * per:(c + 1) * per].reshape(per * S, H)
        m = dict(common)
        m["xT"] = np.ascontiguousarray(shard.T)
        in_maps.append(m)

    if not _NC_CACHE:
        _NC_CACHE.append(build_program())
    nc = _NC_CACHE[0]
    res = run_bass_kernel_spmd(nc, in_maps, core_ids=list(range(n_cores)))
    out = np.empty((B, S, H), dtype=np.float32)
    for c in range(n_cores):
        out[c * per:(c + 1) * per] = res.results[c]["outT"].T.reshape(per, S, H)
    return out


# version 14
# speedup vs baseline: 1.0069x; 1.0069x over previous
import math
import numpy as np
import concourse.bass as bass
import concourse.mybir as mybir
from concourse.bass_utils import run_bass_kernel_spmd

F32 = mybir.dt.float32
BF16 = mybir.dt.bfloat16
U8 = mybir.dt.uint8
AF = mybir.ActivationFunctionType
ALU = mybir.AluOpType

C = 1024
NCH = 8
T = 512
NT = 4096
NTILES = NT // T
NH = 16
HD = 64
EPS_ATTN = 1e-6
EPS_LN1 = 1e-12
EPS_LN = 1e-5
RC = 1.0 / C

ENGS = ("te", "act", "dve", "gp", "sp")
SELF_WAIT = {"act", "dve", "gp"}


class Buf:
    __slots__ = ("w", "r")

    def __init__(self):
        self.w = None
        self.r = []


class Sched:
    def __init__(self):
        self.cnt = {e: 0 for e in ENGS}
        self.dcnt = {}
        self.last_wait = {e: {} for e in ENGS}
        self.prog = {e: [] for e in ENGS}
        self.barrier_toks = []

    def _collect(self, eng, reads, writes, deps):
        need = list(deps) + list(self.barrier_toks)
        for b in reads:
            if b.w is not None:
                need.append(b.w)
        for b in writes:
            need.extend(b.r)
            if b.w is not None:
                need.append(b.w)
        waits = []
        for (sname, v) in need:
            if sname == eng and eng not in SELF_WAIT:
                continue
            lw = self.last_wait[eng].get(sname, 0)
            if v > lw:
                self.last_wait[eng][sname] = v
                waits.append((sname, v))
        return waits

    def op(self, eng, fn, reads=(), writes=(), deps=()):
        waits = self._collect(eng, reads, writes, deps)
        self.cnt[eng] += 1
        tok = (eng, self.cnt[eng])
        self.prog[eng].append((waits, fn, eng, 1))
        for b in reads:
            b.r.append(tok)
        for b in writes:
            b.w = tok
            b.r = []
        return tok

    def dma(self, qeng, dsem, out, in_, reads=(), writes=(), deps=(), **kw):
        sem = dsem
        waits = self._collect(qeng, reads, writes, deps)
        fn = lambda e, out=out, in_=in_, kw=kw: e.dma_start(out=out, in_=in_, **kw)
        self.prog[qeng].append((waits, fn, sem, 16))
        self.dcnt[sem] = self.dcnt.get(sem, 0) + 16
        tok = (sem, self.dcnt[sem])
        for b in reads:
            b.r.append(tok)
        for b in writes:
            b.w = tok
            b.r = []
        return tok

    def barrier(self):
        toks = [(e, self.cnt[e]) for e in ("te", "act", "dve", "gp") if self.cnt[e]]
        toks += [(sn, v) for sn, v in self.dcnt.items() if v]
        self.barrier_toks = toks


def build_program():
    nc = bass.Bass("TRN2", target_bir_lowering=False)

    xT = nc.dram_tensor("xT", [C, NT], F32, kind="ExternalInput").ap()
    outT = nc.dram_tensor("outT", [C, NT], F32, kind="ExternalOutput").ap()
    wnames = ["wq", "wk", "wv", "wd", "wr", "wk2", "wv2"]
    W = {n: nc.dram_tensor(n, [C, C], F32, kind="ExternalInput").ap() for n in wnames}
    econst = nc.dram_tensor("econst", [128, C], F32, kind="ExternalInput").ap()
    gconst = nc.dram_tensor("gconst", [128, C], F32, kind="ExternalInput").ap()
    oneconst = nc.dram_tensor("oneconst", [128, 128], F32, kind="ExternalInput").ap()
    vnames = ["bq", "bk", "bv", "bd", "gamma", "beta", "mur", "muk"]
    VEC = {n: nc.dram_tensor(n, [C], F32, kind="ExternalInput").ap() for n in vnames}

    s = Sched()

    from contextlib import ExitStack
    with ExitStack() as stack:
        en = stack.enter_context
        qall_t = en(nc.sbuf_tensor("qall", [128, NCH * NT], BF16))
        wslot_t = en(nc.sbuf_tensor("wslot", [128, 4 * NCH * C], BF16))
        arena_t = en(nc.sbuf_tensor("arena", [128, 63488], U8))
        G_t = en(nc.sbuf_tensor("G128", [128, 1024], F32))
        ONE_t = en(nc.sbuf_tensor("ONE128", [128, 128], F32))
        AB_t = en(nc.sbuf_tensor("AB", [128, 1024], F32))
        E_t = en(nc.sbuf_tensor("EALL", [128, 1024], F32))
        ZP_t = en(nc.sbuf_tensor("ZP", [128, 512], F32))
        kvb_t = en(nc.sbuf_tensor("kvb", [128, 1024], BF16))
        KS_t = en(nc.sbuf_tensor("KS", [128, 256], BF16))
        consts_t = en(nc.sbuf_tensor("consts", [128, 56], F32))
        carry_t = en(nc.sbuf_tensor("carry", [128, 8], F32))
        onescol_t = en(nc.sbuf_tensor("onescol", [128, 8], BF16))
        cbias_t = en(nc.sbuf_tensor("cbias", [128, 4], F32))
        pm0 = en(nc.psum_tensor("pm0", [128, 512], F32))
        pm1 = en(nc.psum_tensor("pm1", [128, 512], F32))
        pm2 = en(nc.psum_tensor("pm2", [128, 512], F32))
        pm3 = en(nc.psum_tensor("pm3", [128, 512], F32))
        pm4 = en(nc.psum_tensor("pm4", [128, 512], F32))
        pkv_t = en(nc.psum_tensor("pkv", [128, 512], F32))
        pax0 = en(nc.psum_tensor("pax0", [128, 512], F32))
        pax1 = en(nc.psum_tensor("pax1", [128, 512], F32))
        sem_te = en(nc.semaphore("s_te"))
        sem_act = en(nc.semaphore("s_act"))
        sem_dve = en(nc.semaphore("s_dve"))
        sem_gp = en(nc.semaphore("s_gp"))
        block = en(nc.Block())
        qall = qall_t.ap()
        wslot = wslot_t.ap()
        arena = arena_t.ap()
        G = G_t.ap()
        ONE = ONE_t.ap()
        AB = AB_t.ap()
        E = E_t.ap()
        ZP = ZP_t.ap()
        kvb = kvb_t.ap()
        KS = KS_t.ap()
        consts = consts_t.ap()
        carry = carry_t.ap()
        onescol = onescol_t.ap()
        pm = [p.ap() for p in (pm0, pm1, pm2, pm3, pm4)]
        pkv = pkv_t.ap()
        pax = [pax0.ap(), pax1.ap()]
        cbias = cbias_t.ap()
        for _i, _v in enumerate([EPS_ATTN, EPS_LN1, EPS_LN]):
            nc.const_aps.register(F32, _v, cbias[:, _i:_i + 1])

        def AV(off_bytes, cols, dt):
            width = cols * (2 if dt == BF16 else 4)
            return arena[:, off_bytes:off_bytes + width].bitcast(dt)

        xf2 = [AV(ch * 2048, T, F32) for ch in range(NCH)]
        ka = [AV(ch * 1024, T, BF16) for ch in range(NCH)]
        hs = [AV(16384 + ch * 2048, T, F32) for ch in range(NCH)]
        x1 = [AV(32768 + ch * 2048, T, F32) for ch in range(NCH)]
        mix = [AV(49152 + ch * 1024, T, BF16) for ch in range(NCH)]
        mstage = [AV(49152, 1024, F32), AV(53248, 1024, F32)]
        pool = [AV(57344 + i * 2048, T, F32) for i in range(2)]
        bb16 = [AV(61440 + i * 1024, T, BF16) for i in range(2)]
        xb = [[AV(p * 8192 + ch * 1024, T, BF16) for ch in range(NCH)] for p in range(2)]
        xfp = [AV(16384 + i * 2048, T, F32) for i in range(4)]
        ktm = [AV(24576, 1024, BF16), AV(26624, 1024, BF16),
               AV(53248, 1024, BF16)]
        vtm = [AV(28672, 1024, BF16), AV(30720, 1024, BF16),
               AV(55296, 1024, BF16)]
        bias_tm = AV(32768, 2048, F32)
        poolp1 = [AV(40960 + i * 2048, T, F32) for i in range(8)]
        kvacc = AV(57344, 512, F32)
        ksacc = AV(59392, 16, F32)

        a_slot = AB[0:1, 0:512]
        b_slot = AB[0:1, 512:1024]
        stA = AB[0:1, 512:1024]
        stB = pool[0].bitcast(F32)[0:1, 0:512]
        stC = AB[0:1, 0:512]
        zscr = AV(61440, 512, F32)[0:16, 0:512]
        z16 = ZP[0:16, 0:512]

        cn = ["bq", "bd", "mur", "muk", "omr", "omk", "beta"]
        ccol = {n: consts[:, 8 * i:8 * i + 8] for i, n in enumerate(cn)}

        B_q = [[Buf() for _ in range(NCH)] for _ in range(NTILES)]
        B_w = {n: [Buf() for _ in range(NCH)] for n in wnames}
        B_xf2 = [Buf() for _ in range(NCH)]
        B_hs = [Buf() for _ in range(NCH)]
        B_x1 = [Buf() for _ in range(NCH)]
        B_ka = [Buf() for _ in range(NCH)]
        B_mix = [Buf() for _ in range(NCH)]
        B_pool = [Buf() for _ in range(2)]
        B_bb16 = [Buf() for _ in range(2)]
        B_xb = [[Buf() for _ in range(NCH)] for _ in range(2)]
        B_xfp = [Buf() for _ in range(4)]
        B_ktm = [Buf() for _ in range(3)]
        B_vtm = [Buf() for _ in range(3)]

        B_biastm = Buf()
        B_poolp1 = [Buf() for _ in range(8)]
        B_poolp1[6] = B_ktm[2]
        B_poolp1[7] = B_vtm[2]
        B_pm = [Buf() for _ in range(5)]
        B_pkv = Buf()
        B_pax = [Buf() for _ in range(2)]
        B_ksum = [Buf() for _ in range(2)]
        B_kvb = [Buf() for _ in range(2)]
        B_KS = [Buf() for _ in range(2)]
        B_carry = Buf()
        B_G = Buf()
        B_ONE = Buf()
        B_E = Buf()
        B_ZP = Buf()
        B_a = Buf()
        B_b = Buf()
        B_stA = Buf()
        B_stB = Buf()
        B_stC = Buf()
        B_zscr = Buf()
        B_consts = Buf()
        B_onescol = Buf()
        B_cbias = Buf()
        B_out = [Buf() for _ in range(NCH)]

        pm_i = [0]
        pm_depth = [5]

        def next_pm():
            i = pm_i[0] % pm_depth[0]
            pm_i[0] += 1
            return i

        pax_i = [0]
        pax_depth = [2]

        def next_pax():
            if pax_depth[0] == 8:
                return next_pm()
            i = pax_i[0] % pax_depth[0]
            pax_i[0] += 1
            return i

        def ka_access(ch):
            return [B_ka[ch], B_xf2[ch // 2]]

        for i, n in enumerate(["bq", "bd", "mur", "muk"]):
            s.dma("sp", "dset", ccol[n], VEC[n].rearrange("(j p) -> p j", p=128),
                  writes=[B_consts], allow_slow_non_contiguous=True)
        s.dma("sp", "dset", ccol["beta"], VEC["beta"].rearrange("(j p) -> p j", p=128),
              writes=[B_consts], allow_slow_non_contiguous=True)
        s.dma("sp", "dset", E[:], econst, writes=[B_E])
        s.dma("sp", "dset", G[:], gconst, writes=[B_G])
        s.dma("sp", "dset", ONE[:], oneconst, writes=[B_ONE])
        s.dma("sp", "dset", bias_tm[:, 0:1024],
              VEC["bk"][None, :].partition_broadcast(128), writes=[B_biastm])
        s.dma("sp", "dset", bias_tm[:, 1024:2048],
              VEC["bv"][None, :].partition_broadcast(128), writes=[B_biastm])
        for _b in (B_consts, B_biastm, B_E, B_G, B_ONE):
            _b.w = ("dset", s.dcnt["dset"])
        s.op("dve", lambda e: e.tensor_scalar(out=ccol["omr"], in0=ccol["mur"],
             scalar1=-1.0, scalar2=1.0, op0=ALU.mult, op1=ALU.add),
             reads=[B_consts], writes=[B_consts])
        s.op("dve", lambda e: e.tensor_scalar(out=ccol["omk"], in0=ccol["muk"],
             scalar1=-1.0, scalar2=1.0, op0=ALU.mult, op1=ALU.add),
             reads=[B_consts], writes=[B_consts])
        s.op("dve", lambda e: e.memset(AB[:], 0.0),
             writes=[B_a, B_b, B_stA, B_stB, B_stC, B_zscr])
        s.op("dve", lambda e: e.memset(ZP[:], 0.0), writes=[B_ZP])
        s.op("dve", lambda e: e.memset(KS[:], 0.0), writes=[B_KS[0], B_KS[1]])
        s.op("dve", lambda e: e.memset(onescol[:, 0:1], 1.0), writes=[B_onescol])
        for _i, _v in enumerate([EPS_ATTN, EPS_LN1, EPS_LN]):
            s.op("dve", lambda e, _i=_i, _v=_v: e.memset(cbias[:, _i:_i + 1], _v),
                 writes=[B_cbias])

        SLOT = {"wq": 0, "wk": 1, "wv": 2, "wd": 3, "wr": 0, "wk2": 1, "wv2": 2}

        def wsl(n, ch):
            return wslot[:, (SLOT[n] * NCH + ch) * C:(SLOT[n] * NCH + ch + 1) * C]

        def load_weight(n, stage_aps, stage_buf_sets, qeng, sems):
            for ch in range(NCH):
                sa = stage_aps[ch % 2]
                sbs = stage_buf_sets[ch % 2]
                s.dma(qeng, sems[ch % 2], sa, W[n][128 * ch:128 * (ch + 1), :],
                      writes=sbs)
                s.op("dve", lambda e, sa=sa, n=n, ch=ch: e.tensor_scalar(
                    out=wsl(n, ch), in0=sa, scalar1=1.0, scalar2=None, op0=ALU.mult),
                    reads=sbs, writes=[B_w[n][ch]])

        wstage = [AV(40960, 1024, F32), AV(45056, 1024, F32)]
        for n in ["wq", "wk", "wv", "wd"]:
            load_weight(n, wstage,
                        [[B_poolp1[0], B_poolp1[1]], [B_poolp1[2], B_poolp1[3]]],
                        "sp", ("dw0", "dw1"))

        for i in range(NTILES):
            par = i % 2
            batch = i // (NTILES // 2)
            t0 = i * T
            first_of_batch = (i % (NTILES // 2) == 0)
            last_of_batch = (i % (NTILES // 2) == (NTILES // 2) - 1)
            for ch in range(NCH):
                sl = xfp[ch % 4]
                s.dma("sp", "dx%d" % (ch % 4), sl,
                      xT[128 * ch:128 * (ch + 1), t0:t0 + T],
                      writes=[B_xfp[ch % 4]])
                s.op("dve", lambda e, sl=sl, par=par, ch=ch: e.tensor_scalar(
                    out=xb[par][ch], in0=sl, scalar1=1.0, scalar2=None, op0=ALU.mult),
                    reads=[B_xfp[ch % 4]], writes=[B_xb[par][ch]])
            for dch in range(NCH):
                b = next_pm()
                for cch in range(NCH):
                    s.op("te", lambda e, b=b, dch=dch, cch=cch, par=par: e.matmul(
                        pm[b][:, 0:T], wsl("wq", cch)[:, 128 * dch:128 * (dch + 1)],
                        xb[par][cch], start=(cch == 0), stop=(cch == NCH - 1),
                        skip_group_check=True),
                        reads=[B_w["wq"][cch], B_xb[par][cch]], writes=[B_pm[b]])
                ie, ir = (dch % 3) * 2, (dch % 3) * 2 + 1
                pe, pr = poolp1[ie], poolp1[ir]
                bq_ap = ccol["bq"][:, dch:dch + 1]
                s.op("act", lambda e, b=b, pe=pe, bq_ap=bq_ap: e.activation(
                    out=pe, in_=pm[b][:, 0:T], func=AF.Exp, bias=bq_ap, scale=1.0),
                    reads=[B_pm[b], B_consts], writes=[B_poolp1[ie]])
                s.op("act", lambda e, b=b, pr=pr, bq_ap=bq_ap: e.activation(
                    out=pr, in_=pm[b][:, 0:T], func=AF.Relu, bias=bq_ap, scale=1.0),
                    reads=[B_pm[b], B_consts], writes=[B_poolp1[ir]])
                qdst = qall[:, dch * NT + t0:dch * NT + t0 + T]
                s.op("dve", lambda e, pe=pe, pr=pr, qdst=qdst: e.scalar_tensor_tensor(
                    out=qdst, in0=pe, scalar=1.0, in1=pr, op0=ALU.min, op1=ALU.add),
                    reads=[B_poolp1[ie], B_poolp1[ir]], writes=[B_q[i][dch]])
            for sub in range(T // 128):
                p2 = (i * (T // 128) + sub) % 3
                first_sub = first_of_batch and sub == 0
                last_sub = last_of_batch and sub == (T // 128) - 1
                for dh in range(2):
                    bk_ = next_pm()
                    bv_ = next_pm()
                    for cch in range(NCH):
                        lhs = xb[par][cch][:, 128 * sub:128 * (sub + 1)]
                        s.op("te", lambda e, bk_=bk_, lhs=lhs, cch=cch, dh=dh: e.matmul(
                            pm[bk_][:, 0:T], lhs,
                            wsl("wk", cch)[:, 512 * dh:512 * (dh + 1)],
                            start=(cch == 0), stop=(cch == NCH - 1),
                            skip_group_check=True),
                            reads=[B_xb[par][cch], B_w["wk"][cch]], writes=[B_pm[bk_]])
                    for cch in range(NCH):
                        lhs = xb[par][cch][:, 128 * sub:128 * (sub + 1)]
                        s.op("te", lambda e, bv_=bv_, lhs=lhs, cch=cch, dh=dh: e.matmul(
                            pm[bv_][:, 0:T], lhs,
                            wsl("wv", cch)[:, 512 * dh:512 * (dh + 1)],
                            start=(cch == 0), stop=(cch == NCH - 1),
                            skip_group_check=True),
                            reads=[B_xb[par][cch], B_w["wv"][cch]], writes=[B_pm[bv_]])
                    iu, ie, ir = 3 * dh, 3 * dh + 1, 3 * dh + 2
                    pu, pe, pr = poolp1[iu], poolp1[ie], poolp1[ir]
                    bsl = bias_tm[:, 512 * dh:512 * (dh + 1)]
                    s.op("dve", lambda e, bk_=bk_, pu=pu, bsl=bsl: e.tensor_tensor(
                        out=pu, in0=pm[bk_][:, 0:T], in1=bsl, op=ALU.add),
                        reads=[B_pm[bk_], B_biastm], writes=[B_poolp1[iu]])
                    s.op("act", lambda e, pu=pu, pe=pe: e.activation(
                        out=pe, in_=pu, func=AF.Exp),
                        reads=[B_poolp1[iu]], writes=[B_poolp1[ie]])
                    s.op("act", lambda e, pu=pu, pr=pr: e.activation(
                        out=pr, in_=pu, func=AF.Relu),
                        reads=[B_poolp1[iu]], writes=[B_poolp1[ir]])
                    kdst = ktm[p2][:, 512 * dh:512 * (dh + 1)]
                    s.op("dve", lambda e, pe=pe, pr=pr, kdst=kdst:
                         e.scalar_tensor_tensor(
                        out=kdst, in0=pe, scalar=1.0, in1=pr, op0=ALU.min,
                        op1=ALU.add),
                        reads=[B_poolp1[ie], B_poolp1[ir]], writes=[B_ktm[p2]])
                    vdst = vtm[p2][:, 512 * dh:512 * (dh + 1)]
                    bsv = bias_tm[:, 1024 + 512 * dh:1024 + 512 * (dh + 1)]
                    s.op("dve", lambda e, bv_=bv_, vdst=vdst, bsv=bsv: e.tensor_tensor(
                        out=vdst, in0=pm[bv_][:, 0:T], in1=bsv, op=ALU.add),
                        reads=[B_pm[bv_], B_biastm], writes=[B_vtm[p2]])
                skv = (i * (T // 128) + sub) % 2
                kvb_ps = pkv if skv == 0 else pax[1]
                kvb_buf = B_pkv if skv == 0 else B_pax[1]
                kscols = 0 if skv == 0 else 8
                for ch in range(NCH):
                    for hh in range(2):
                        lhs = ktm[p2][:, ch * 128 + hh * 64:ch * 128 + hh * 64 + 64]
                        rhs = vtm[p2][:, ch * 128 + hh * 64:ch * 128 + hh * 64 + 64]
                        dst = kvb_ps[hh * 64:(hh + 1) * 64, ch * 64:(ch + 1) * 64]
                        tp = (0, 64) if hh == 1 else None
                        s.op("te", lambda e, lhs=lhs, rhs=rhs, dst=dst, tp=tp:
                             e.matmul(dst, lhs, rhs, start=True, stop=True,
                                      tile_position=tp, skip_group_check=True),
                            reads=[B_ktm[p2], B_vtm[p2]], writes=[kvb_buf])
                    s.op("te", lambda e, ch=ch, p2=p2, kscols=kscols: e.matmul(
                        pax[0][:, kscols + ch:kscols + ch + 1],
                        ktm[p2][:, ch * 128:(ch + 1) * 128], onescol[:, 0:1],
                        start=(ch == 0), stop=(ch == NCH - 1),
                        skip_group_check=True),
                        reads=[B_ktm[p2], B_onescol], writes=[B_ksum[skv]])
                if first_sub:
                    s.op("dve", lambda e, kvb_ps=kvb_ps: e.tensor_copy(
                         kvacc, kvb_ps[:, 0:512]),
                         reads=[kvb_buf], writes=[B_pool[0]])
                    s.op("dve", lambda e, kscols=kscols: e.tensor_copy(
                         ksacc[:, 0:8], pax[0][:, kscols:kscols + 8]),
                         reads=[B_ksum[skv]], writes=[B_pool[1]])
                else:
                    s.op("dve", lambda e, kvb_ps=kvb_ps: e.tensor_tensor(
                         out=kvacc, in0=kvacc,
                         in1=kvb_ps[:, 0:512], op=ALU.add),
                         reads=[kvb_buf, B_pool[0]], writes=[B_pool[0]])
                    s.op("dve", lambda e, kscols=kscols: e.tensor_tensor(
                         out=ksacc[:, 0:8],
                         in0=ksacc[:, 0:8], in1=pax[0][:, kscols:kscols + 8],
                         op=ALU.add),
                         reads=[B_ksum[skv]], writes=[B_pool[1]])
            if last_of_batch:
                s.op("dve", lambda e, batch=batch: e.tensor_scalar(
                    out=kvb[:, batch * 512:(batch + 1) * 512], in0=kvacc,
                    scalar1=1.0, scalar2=None, op0=ALU.mult),
                    reads=[B_pool[0]], writes=[B_kvb[batch]])
                for ch in range(NCH):
                    for hh in range(2):
                        h = 2 * ch + hh
                        ksrc = ksacc[hh * 64:(hh + 1) * 64, ch:ch + 1]
                        dst = KS[hh * 64:(hh + 1) * 64,
                                 batch * 128 + ch * 16 + h:
                                 batch * 128 + ch * 16 + h + 1]
                        s.op("act", lambda e, ksrc=ksrc, dst=dst: e.activation(
                            out=dst, in_=ksrc, func=AF.Copy, bias=EPS_ATTN,
                            scale=1.0),
                            reads=[B_pool[1], B_cbias], writes=[B_KS[batch]])

        s.barrier()
        pax.append(pkv)
        B_pax.append(B_pkv)
        pax_depth[0] = 3

        mst_bufs = [[B_mix[0], B_mix[1], B_mix[2], B_mix[3]],
                    [B_mix[4], B_mix[5], B_mix[6], B_mix[7]]]

        def emit_morph():
            for n in ["wr", "wk2", "wv2"]:
                load_weight(n, mstage, mst_bufs, "sp", ("dm0", "dm1"))

        def ln_chain(bs, bq_, eps):
            s.op("act", lambda e, bs=bs: e.activation(out=stA, in_=pm[bs][0:1, 0:T],
                 func=AF.Square, scale=RC),
                 reads=[B_pm[bs]], writes=[B_b])
            s.op("dve", lambda e, bq_=bq_: e.scalar_tensor_tensor(
                 out=stB, in0=pm[bq_][0:1, 0:T],
                 scalar=RC, in1=stA, op0=ALU.mult, op1=ALU.subtract),
                 reads=[B_pm[bq_], B_b], writes=[B_pool[0]])
            s.op("act", lambda e, eps=eps: e.activation(out=stC, in_=stB,
                 func=AF.Ln, bias=eps, scale=1.0),
                 reads=[B_pool[0], B_cbias], writes=[B_a])
            s.op("act", lambda e: e.activation(out=a_slot, in_=stC,
                 func=AF.Exp, scale=-0.5),
                 reads=[B_a], writes=[B_a])
            s.op("dve", lambda e, bs=bs: e.scalar_tensor_tensor(
                 out=b_slot, in0=pm[bs][0:1, 0:T],
                 scalar=-RC, in1=a_slot, op0=ALU.mult, op1=ALU.mult),
                 reads=[B_pm[bs], B_a], writes=[B_b])

        def stat_sum_pre(a, abuf, ch):
            s.op("act", lambda e, a=a, o=mix[ch]: e.activation(
                out=o, in_=a, func=AF.Copy), reads=[abuf], writes=[B_mix[ch]])

        def ln_stats(src_aps, src_bufs, prestaged):
            bs = next_pm()
            bq_ = next_pm()
            if not prestaged:
                for ch in range(NCH):
                    stat_sum_pre(src_aps[ch], src_bufs[ch], ch)
            for ch in range(NCH):
                s.op("te", lambda e, bs=bs, cs=mix[ch], ch=ch: e.matmul(
                    pm[bs][0:1, 0:T], onescol[:, 0:1], cs,
                    start=(ch == 0), stop=(ch == NCH - 1), skip_group_check=True),
                    reads=[B_mix[ch], B_onescol], writes=[B_pm[bs]])
                s.op("act", lambda e, cq=mix[ch], a=src_aps[ch]: e.activation(
                    out=cq, in_=a, func=AF.Square),
                    reads=[src_bufs[ch], B_mix[ch]], writes=[B_mix[ch]])
            for ch in range(NCH):
                s.op("te", lambda e, bq_=bq_, cq=mix[ch], ch=ch: e.matmul(
                    pm[bq_][0:1, 0:T], onescol[:, 0:1], cq,
                    start=(ch == 0), stop=(ch == NCH - 1), skip_group_check=True),
                    reads=[B_mix[ch], B_onescol], writes=[B_pm[bq_]])
            return bs, bq_

        def xf2_loads(i):
            t0 = i * T
            for ch in range(NCH):
                wl = [B_xf2[ch]]
                if ch < 4:
                    wl += [B_ka[2 * ch], B_ka[2 * ch + 1]]
                s.dma("sp", "df%d" % ch, xf2[ch],
                      xT[128 * ch:128 * (ch + 1), t0:t0 + T], writes=wl)

        def attn_early(i):
            batch = i // (NTILES // 2)
            t0 = i * T
            bd_ = next_pax()
            for ch in range(NCH):
                lhs = KS[:, batch * 128 + ch * 16:batch * 128 + (ch + 1) * 16]
                rhs = qall[:, ch * NT + t0:ch * NT + t0 + T]
                s.op("te", lambda e, bd_=bd_, lhs=lhs, rhs=rhs, ch=ch: e.matmul(
                    pax[bd_][0:16, 0:T], lhs, rhs, start=(ch == 0),
                    stop=(ch == NCH - 1), skip_group_check=True),
                    reads=[B_KS[batch], B_q[i][ch]], writes=[B_pax[bd_]])
            s.op("act", lambda e, bd_=bd_: e.activation(out=z16,
                 in_=pax[bd_][0:16, 0:T], func=AF.Ln),
                 reads=[B_pax[bd_]], writes=[B_ZP])
            s.op("act", lambda e: e.activation(out=z16, in_=z16, func=AF.Exp,
                 scale=-1.0),
                 reads=[B_ZP], writes=[B_ZP])
            s.op("dve", lambda e, bd_=bd_: e.tensor_tensor(out=zscr, in0=z16,
                 in1=pax[bd_][0:16, 0:T], op=ALU.mult),
                 reads=[B_ZP, B_pax[bd_]], writes=[B_bb16[0], B_bb16[1]])
            s.op("dve", lambda e: e.tensor_scalar(out=zscr, in0=zscr, scalar1=-1.0,
                 scalar2=2.0, op0=ALU.mult, op1=ALU.add),
                 reads=[B_bb16[0]], writes=[B_bb16[0], B_bb16[1]])
            s.op("dve", lambda e: e.tensor_tensor(out=z16, in0=z16, in1=zscr,
                 op=ALU.mult),
                 reads=[B_bb16[0], B_ZP], writes=[B_ZP])
            zbanks = {}

            def emit_zb(ch):
                bz = next_pax()
                zbanks[ch] = bz
                s.op("te", lambda e, bz=bz, ch=ch: e.matmul(
                    pax[bz][:, 0:T], E[:, ch * 128:(ch + 1) * 128], ZP[:, 0:T],
                    start=True, stop=True, skip_group_check=True),
                    reads=[B_E, B_ZP], writes=[B_pax[bz]])

            emit_zb(0)
            emit_zb(1)
            for ch in range(NCH):
                if ch + 2 < NCH:
                    emit_zb(ch + 2)
                bz = zbanks[ch]
                qsl = qall[:, ch * NT + t0:ch * NT + t0 + T]
                s.op("dve", lambda e, bz=bz, ch=ch, qsl=qsl: e.tensor_tensor(
                    out=bb16[ch % 2], in0=qsl, in1=pax[bz][:, 0:T], op=ALU.mult),
                    reads=[B_q[i][ch], B_pax[bz]], writes=[B_bb16[ch % 2]])
                bc = next_pm()
                for hh in range(2):
                    lhs = kvb[hh * 64:(hh + 1) * 64,
                              batch * 512 + ch * 64:batch * 512 + (ch + 1) * 64]
                    rhs = bb16[ch % 2][hh * 64:(hh + 1) * 64, 0:T]
                    dst = pm[bc][hh * 64:(hh + 1) * 64, 0:T]
                    tp = (64, 64) if hh == 1 else None
                    s.op("te", lambda e, lhs=lhs, rhs=rhs, dst=dst, tp=tp: e.matmul(
                        dst, lhs, rhs, start=True, stop=True, tile_position=tp,
                        skip_group_check=True),
                        reads=[B_kvb[batch], B_bb16[ch % 2]], writes=[B_pm[bc]])
                s.op("act", lambda e, bc=bc, qsl=qsl: e.activation(
                    out=qsl, in_=pm[bc][:, 0:T], func=AF.Copy),
                    reads=[B_pm[bc]], writes=[B_q[i][ch]])

        wd_banks = {}

        def wd_mms(i, dchs):
            t0 = i * T
            for dch in dchs:
                b = next_pm()
                wd_banks[dch] = b
                for cch in range(NCH):
                    rhs = qall[:, cch * NT + t0:cch * NT + t0 + T]
                    s.op("te", lambda e, b=b, dch=dch, cch=cch, rhs=rhs: e.matmul(
                        pm[b][:, 0:T], wsl("wd", cch)[:, 128 * dch:128 * (dch + 1)],
                        rhs, start=(cch == 0), stop=(cch == NCH - 1),
                        skip_group_check=True),
                        reads=[B_w["wd"][cch], B_q[i][cch]], writes=[B_pm[b]])

        def wd_evac(i, dch):
            b = wd_banks[dch]
            s.op("dve", lambda e, b=b, dch=dch: e.scalar_tensor_tensor(
                out=hs[dch], in0=pm[b][:, 0:T], scalar=ccol["bd"][:, dch:dch + 1],
                in1=xf2[dch], op0=ALU.add, op1=ALU.add),
                reads=[B_pm[b], B_xf2[dch], B_consts], writes=[B_hs[dch]])
            stat_sum_pre(hs[dch], B_hs[dch], dch)

        def wd_block(i):
            for dch in range(NCH):
                wd_mms(i, [dch])
                wd_evac(i, dch)

        def ln1_block(i):
            bs, bq_ = ln_stats(hs, B_hs, True)
            ln_chain(bs, bq_, EPS_LN1)
            for ch in range(NCH):
                ba = next_pax()
                bb = next_pax()
                gl = G[:, ch * 128:(ch + 1) * 128]
                s.op("te", lambda e, ba=ba, gl=gl: e.matmul(
                    pax[ba][:, 0:T], gl, AB[:, 0:512], start=True, stop=True,
                    skip_group_check=True),
                    reads=[B_G, B_a], writes=[B_pax[ba]])
                s.op("te", lambda e, bb=bb, gl=gl: e.matmul(
                    pax[bb][:, 0:T], gl, AB[:, 512:1024], start=True, stop=True,
                    skip_group_check=True),
                    reads=[B_G, B_b], writes=[B_pax[bb]])
                s.op("dve", lambda e, ba=ba, ch=ch: e.tensor_tensor(
                    out=hs[ch], in0=hs[ch], in1=pax[ba][:, 0:T], op=ALU.mult),
                    reads=[B_hs[ch], B_pax[ba]], writes=[B_hs[ch]])
                s.op("dve", lambda e, bb=bb, ch=ch: e.scalar_tensor_tensor(
                    out=xf2[ch], in0=pax[bb][:, 0:T],
                    scalar=ccol["beta"][:, ch:ch + 1], in1=xf2[ch],
                    op0=ALU.add, op1=ALU.add),
                    reads=[B_pax[bb], B_xf2[ch], B_consts], writes=[B_xf2[ch]])
                s.op("gp", lambda e, ch=ch: e.tensor_tensor(
                    out=hs[ch], in0=hs[ch], in1=xf2[ch], op=ALU.add),
                    reads=[B_hs[ch], B_xf2[ch]], writes=[B_hs[ch]])
                stat_sum_pre(hs[ch], B_hs[ch], ch)

        def ln2_block(i):
            bs, bq_ = ln_stats(hs, B_hs, True)
            ln_chain(bs, bq_, EPS_LN)
            ba = next_pax()
            bb = next_pax()
            s.op("te", lambda e, ba=ba: e.matmul(pax[ba][:, 0:T], ONE[:, 0:128],
                 AB[:, 0:512], start=True, stop=True, skip_group_check=True),
                 reads=[B_ONE, B_a], writes=[B_pax[ba]])
            s.op("te", lambda e, bb=bb: e.matmul(pax[bb][:, 0:T], ONE[:, 0:128],
                 AB[:, 512:1024], start=True, stop=True, skip_group_check=True),
                 reads=[B_ONE, B_b], writes=[B_pax[bb]])
            for ch in range(NCH):
                s.op("dve", lambda e, ba=ba, ch=ch: e.tensor_tensor(
                    out=x1[ch], in0=hs[ch], in1=pax[ba][:, 0:T], op=ALU.mult),
                    reads=[B_hs[ch], B_pax[ba]], writes=[B_x1[ch]])
                s.op("dve", lambda e, bb=bb, ch=ch: e.tensor_tensor(
                    out=x1[ch], in0=x1[ch], in1=pax[bb][:, 0:T], op=ALU.add),
                    reads=[B_x1[ch], B_pax[bb]], writes=[B_x1[ch]])

        def mk_mix(mu_name, om_name):
            for ch in range(NCH):
                po = pool[ch % 2]
                om = ccol[om_name][:, ch:ch + 1]
                s.op("act", lambda e, ch=ch, po=po, om=om: e.activation(
                    out=po[:, 1:T], in_=x1[ch][:, 0:T - 1], func=AF.Copy,
                    scale=om),
                    reads=[B_x1[ch], B_consts], writes=[B_pool[ch % 2]])
                s.op("act", lambda e, ch=ch, po=po, om=om: e.activation(
                    out=po[:, 0:1], in_=carry[:, ch:ch + 1], func=AF.Copy,
                    scale=om),
                    reads=[B_carry, B_consts], writes=[B_pool[ch % 2]])
                mu = ccol[mu_name][:, ch:ch + 1]
                s.op("dve", lambda e, ch=ch, po=po, mu=mu: e.scalar_tensor_tensor(
                    out=mix[ch], in0=x1[ch], scalar=mu, in1=po,
                    op0=ALU.mult, op1=ALU.add),
                    reads=[B_x1[ch], B_pool[ch % 2], B_consts],
                    writes=[B_mix[ch]])

        def ffn1(i):
            if i % (NTILES // 2) == 0:
                s.op("dve", lambda e: e.memset(carry[:], 0.0), writes=[B_carry])
            mk_mix("muk", "omk")
            for dch in range(NCH):
                b = next_pm()
                for cch in range(NCH):
                    s.op("te", lambda e, b=b, dch=dch, cch=cch: e.matmul(
                        pm[b][:, 0:T], wsl("wk2", cch)[:, 128 * dch:128 * (dch + 1)],
                        mix[cch], start=(cch == 0), stop=(cch == NCH - 1),
                        skip_group_check=True),
                        reads=[B_w["wk2"][cch], B_mix[cch]], writes=[B_pm[b]])
                po = pool[dch % 2]
                s.op("dve", lambda e, b=b, po=po: e.tensor_scalar(
                    out=po, in0=pm[b][:, 0:T], scalar1=0.0, scalar2=None,
                    op0=ALU.max),
                    reads=[B_pm[b]], writes=[B_pool[dch % 2]])
                s.op("act", lambda e, po=po, dch=dch: e.activation(
                    out=ka[dch], in_=po, func=AF.Square),
                    reads=[B_pool[dch % 2]], writes=ka_access(dch))

        def ffn2(i):
            mk_mix("mur", "omr")
            for ch in range(NCH):
                s.op("dve", lambda e, ch=ch: e.tensor_copy(
                    carry[:, ch:ch + 1], x1[ch][:, T - 1:T]),
                    reads=[B_x1[ch]], writes=[B_carry])
            for dch in range(NCH):
                br_ = next_pm()
                for cch in range(NCH):
                    s.op("te", lambda e, br_=br_, dch=dch, cch=cch: e.matmul(
                        pm[br_][:, 0:T], wsl("wr", cch)[:, 128 * dch:128 * (dch + 1)],
                        mix[cch], start=(cch == 0), stop=(cch == NCH - 1),
                        skip_group_check=True),
                        reads=[B_w["wr"][cch], B_mix[cch]], writes=[B_pm[br_]])
                bv_ = next_pm()
                for cch in range(NCH):
                    s.op("te", lambda e, bv_=bv_, dch=dch, cch=cch: e.matmul(
                        pm[bv_][:, 0:T], wsl("wv2", cch)[:, 128 * dch:128 * (dch + 1)],
                        ka[cch], start=(cch == 0), stop=(cch == NCH - 1),
                        skip_group_check=True),
                        reads=[B_w["wv2"][cch], B_ka[cch], B_xf2[cch // 2]],
                        writes=[B_pm[bv_]])
                pth = pool[dch % 2]
                prr = pool[(dch + 1) % 2]
                s.op("act", lambda e, br_=br_, pth=pth: e.activation(
                    out=pth, in_=pm[br_][:, 0:T], func=AF.Tanh, scale=0.5),
                    reads=[B_pm[br_]], writes=[B_pool[dch % 2]])
                s.op("dve", lambda e, pth=pth, prr=prr: e.tensor_scalar(
                    out=prr, in0=pth, scalar1=0.5, scalar2=0.5, op0=ALU.mult,
                    op1=ALU.add),
                    reads=[B_pool[dch % 2]], writes=[B_pool[(dch + 1) % 2]])
                s.op("dve", lambda e, bv_=bv_, prr=prr, pth=pth: e.tensor_tensor(
                    out=pth, in0=pm[bv_][:, 0:T], in1=prr, op=ALU.mult),
                    reads=[B_pm[bv_], B_pool[(dch + 1) % 2]],
                    writes=[B_pool[dch % 2]])
                s.op("gp", lambda e, pth=pth, dch=dch: e.tensor_tensor(
                    out=hs[dch], in0=pth, in1=x1[dch], op=ALU.add),
                    reads=[B_pool[dch % 2], B_x1[dch]], writes=[B_hs[dch]])

        def ln3_out(i, overlap_next):
            t0 = i * T
            bs, bq_ = ln_stats(hs, B_hs, False)
            if overlap_next:
                wd_mms(i + 1, [0, 1, 2])
            ln_chain(bs, bq_, EPS_LN)
            ba = next_pax()
            bb = next_pax()
            s.op("te", lambda e, ba=ba: e.matmul(pax[ba][:, 0:T], ONE[:, 0:128],
                 AB[:, 0:512], start=True, stop=True, skip_group_check=True),
                 reads=[B_ONE, B_a], writes=[B_pax[ba]])
            s.op("te", lambda e, bb=bb: e.matmul(pax[bb][:, 0:T], ONE[:, 0:128],
                 AB[:, 512:1024], start=True, stop=True, skip_group_check=True),
                 reads=[B_ONE, B_b], writes=[B_pax[bb]])
            for ch in range(NCH):
                s.op("dve", lambda e, ba=ba, ch=ch: e.tensor_tensor(
                    out=x1[ch], in0=hs[ch], in1=pax[ba][:, 0:T], op=ALU.mult),
                    reads=[B_hs[ch], B_pax[ba]], writes=[B_x1[ch]])
                s.op("dve", lambda e, bb=bb, ch=ch: e.tensor_tensor(
                    out=x1[ch], in0=x1[ch], in1=pax[bb][:, 0:T], op=ALU.add),
                    reads=[B_x1[ch], B_pax[bb]], writes=[B_x1[ch]])
                s.dma("sp", "do%d" % ch, outT[128 * ch:128 * (ch + 1), t0:t0 + T],
                      x1[ch], reads=[B_x1[ch]], writes=[B_out[ch]])

        xf2_loads(0)
        emit_morph()
        attn_early(0)
        wd_block(0)
        for i in range(NTILES):
            ln1_block(i)
            if i + 1 < NTILES:
                attn_early(i + 1)
            ln2_block(i)
            ffn1(i)
            ffn2(i)
            if i + 1 < NTILES:
                xf2_loads(i + 1)
            ln3_out(i, i + 1 < NTILES)
            if i + 1 < NTILES:
                for dch in (0, 1, 2):
                    wd_evac(i + 1, dch)
                for dch in (3, 4, 5, 6, 7):
                    wd_mms(i + 1, [dch])
                    wd_evac(i + 1, dch)

        sems = {"te": sem_te, "act": sem_act, "dve": sem_dve, "gp": sem_gp}
        for dname in sorted(s.dcnt):
            sems[dname] = en(nc.semaphore(dname))
        out_finals = [("do%d" % ch, s.dcnt["do%d" % ch]) for ch in range(NCH)]

        def play(eng_name, eng):
            for (waits, fn, inc_sem, inc_amt) in s.prog[eng_name]:
                for (sname, v) in waits:
                    eng.wait_ge(sems[sname], v)
                fn(eng).then_inc(sems[inc_sem], inc_amt)

        @block.tensor
        def _(e):
            play("te", e)

        @block.scalar
        def _(e):
            play("act", e)

        @block.vector
        def _(e):
            play("dve", e)

        @block.sync
        def _(e):
            play("sp", e)

        @block.gpsimd
        def _(e):
            play("gp", e)
            for dname, v in out_finals:
                e.wait_ge(sems[dname], v)

    return nc


_NC_CACHE = []


def kernel(**inputs):
    x = np.asarray(inputs["L0_attn_input_tensor"], dtype=np.float32)
    B, S, H = x.shape
    n_cores = 8
    per = B // n_cores

    wmap = {"wq": "L0_attn_Wq", "wk": "L0_attn_Wk", "wv": "L0_attn_Wv",
            "wd": "L0_attn_Wd", "wr": "L0_ffn_W_r", "wk2": "L0_ffn_W_k",
            "wv2": "L0_ffn_W_v"}
    vmap = {"bq": "L0_attn_bq", "bk": "L0_attn_bk", "bv": "L0_attn_bv",
            "bd": "L0_attn_bd", "gamma": "L0_attn_gamma", "beta": "L0_attn_beta",
            "mur": "L0_ffn_mu_r", "muk": "L0_ffn_mu_k"}
    common = {}
    for k, src in wmap.items():
        common[k] = np.ascontiguousarray(np.asarray(inputs[src], dtype=np.float32).T)
    for k, src in vmap.items():
        common[k] = np.ascontiguousarray(np.asarray(inputs[src], dtype=np.float32))

    ecst = np.zeros((128, H), dtype=np.float32)
    for ch in range(8):
        ecst[2 * ch, ch * 128:ch * 128 + 64] = 1.0
        ecst[2 * ch + 1, ch * 128 + 64:ch * 128 + 128] = 1.0
    gcst = np.zeros((128, H), dtype=np.float32)
    gcst[0, :] = common["gamma"]
    ocst = np.zeros((128, 128), dtype=np.float32)
    ocst[0, :] = 1.0
    common["econst"] = ecst
    common["gconst"] = gcst
    common["oneconst"] = ocst

    in_maps = []
    for c in range(n_cores):
        shard = x[c * per:(c + 1) * per].reshape(per * S, H)
        m = dict(common)
        m["xT"] = np.ascontiguousarray(shard.T)
        in_maps.append(m)

    if not _NC_CACHE:
        _NC_CACHE.append(build_program())
    nc = _NC_CACHE[0]
    res = run_bass_kernel_spmd(nc, in_maps, core_ids=list(range(n_cores)))
    out = np.empty((B, S, H), dtype=np.float32)
    for c in range(n_cores):
        out[c * per:(c + 1) * per] = res.results[c]["outT"].T.reshape(per, S, H)
    return out


# version 15 (final)
# speedup vs baseline: 1.0719x; 1.0645x over previous
import math
import numpy as np
import concourse.bass as bass
import concourse.mybir as mybir
from concourse.bass_utils import run_bass_kernel_spmd

F32 = mybir.dt.float32
BF16 = mybir.dt.bfloat16
U8 = mybir.dt.uint8
AF = mybir.ActivationFunctionType
ALU = mybir.AluOpType

C = 1024
NCH = 8
T = 512
NT = 4096
NTILES = NT // T
NH = 16
HD = 64
EPS_ATTN = 1e-6
EPS_LN1 = 1e-12
EPS_LN = 1e-5
RC = 1.0 / C

ENGS = ("te", "act", "dve", "gp", "sp")
SELF_WAIT = {"act", "dve", "gp"}


class Buf:
    __slots__ = ("w", "r")

    def __init__(self):
        self.w = None
        self.r = []


class Sched:
    def __init__(self):
        self.cnt = {e: 0 for e in ENGS}
        self.dcnt = {}
        self.last_wait = {e: {} for e in ENGS}
        self.prog = {e: [] for e in ENGS}
        self.barrier_toks = []

    def _collect(self, eng, reads, writes, deps):
        need = list(deps) + list(self.barrier_toks)
        for b in reads:
            if b.w is not None:
                need.append(b.w)
        for b in writes:
            need.extend(b.r)
            if b.w is not None:
                need.append(b.w)
        waits = []
        for (sname, v) in need:
            if sname == eng and eng not in SELF_WAIT:
                continue
            lw = self.last_wait[eng].get(sname, 0)
            if v > lw:
                self.last_wait[eng][sname] = v
                waits.append((sname, v))
        return waits

    def op(self, eng, fn, reads=(), writes=(), deps=()):
        waits = self._collect(eng, reads, writes, deps)
        self.cnt[eng] += 1
        tok = (eng, self.cnt[eng])
        self.prog[eng].append((waits, fn, eng, 1))
        for b in reads:
            b.r.append(tok)
        for b in writes:
            b.w = tok
            b.r = []
        return tok

    def dma(self, qeng, dsem, out, in_, reads=(), writes=(), deps=(), **kw):
        sem = dsem
        waits = self._collect(qeng, reads, writes, deps)
        fn = lambda e, out=out, in_=in_, kw=kw: e.dma_start(out=out, in_=in_, **kw)
        self.prog[qeng].append((waits, fn, sem, 16))
        self.dcnt[sem] = self.dcnt.get(sem, 0) + 16
        tok = (sem, self.dcnt[sem])
        for b in reads:
            b.r.append(tok)
        for b in writes:
            b.w = tok
            b.r = []
        return tok

    def barrier(self):
        toks = [(e, self.cnt[e]) for e in ("te", "act", "dve", "gp") if self.cnt[e]]
        toks += [(sn, v) for sn, v in self.dcnt.items() if v]
        self.barrier_toks = toks


def build_program():
    nc = bass.Bass("TRN2", target_bir_lowering=False)

    xT = nc.dram_tensor("xT", [C, NT], F32, kind="ExternalInput").ap()
    outT = nc.dram_tensor("outT", [C, NT], F32, kind="ExternalOutput").ap()
    wnames = ["wq", "wk", "wv", "wd", "wr", "wk2", "wv2"]
    W = {n: nc.dram_tensor(n, [C, C], F32, kind="ExternalInput").ap() for n in wnames}
    econst = nc.dram_tensor("econst", [128, C], F32, kind="ExternalInput").ap()
    gconst = nc.dram_tensor("gconst", [128, C], F32, kind="ExternalInput").ap()
    oneconst = nc.dram_tensor("oneconst", [128, 128], F32, kind="ExternalInput").ap()
    vnames = ["bq", "bk", "bv", "bd", "gamma", "beta", "mur", "muk"]
    VEC = {n: nc.dram_tensor(n, [C], F32, kind="ExternalInput").ap() for n in vnames}

    s = Sched()

    from contextlib import ExitStack
    with ExitStack() as stack:
        en = stack.enter_context
        qall_t = en(nc.sbuf_tensor("qall", [128, NCH * NT], BF16))
        wslot_t = en(nc.sbuf_tensor("wslot", [128, 4 * NCH * C], BF16))
        arena_t = en(nc.sbuf_tensor("arena", [128, 63488], U8))
        G_t = en(nc.sbuf_tensor("G128", [128, 1024], F32))
        ONE_t = en(nc.sbuf_tensor("ONE128", [128, 128], F32))
        AB_t = en(nc.sbuf_tensor("AB", [128, 1024], F32))
        E_t = en(nc.sbuf_tensor("EALL", [128, 1024], F32))
        ZP_t = en(nc.sbuf_tensor("ZP", [128, 512], F32))
        kvb_t = en(nc.sbuf_tensor("kvb", [128, 1024], BF16))
        KS_t = en(nc.sbuf_tensor("KS", [128, 256], BF16))
        consts_t = en(nc.sbuf_tensor("consts", [128, 56], F32))
        carry_t = en(nc.sbuf_tensor("carry", [128, 8], F32))
        onescol_t = en(nc.sbuf_tensor("onescol", [128, 8], BF16))
        cbias_t = en(nc.sbuf_tensor("cbias", [128, 4], F32))
        pm0 = en(nc.psum_tensor("pm0", [128, 512], F32))
        pm1 = en(nc.psum_tensor("pm1", [128, 512], F32))
        pm2 = en(nc.psum_tensor("pm2", [128, 512], F32))
        pm3 = en(nc.psum_tensor("pm3", [128, 512], F32))
        pm4 = en(nc.psum_tensor("pm4", [128, 512], F32))
        pkv_t = en(nc.psum_tensor("pkv", [128, 512], F32))
        pax0 = en(nc.psum_tensor("pax0", [128, 512], F32))
        pax1 = en(nc.psum_tensor("pax1", [128, 512], F32))
        sem_te = en(nc.semaphore("s_te"))
        sem_act = en(nc.semaphore("s_act"))
        sem_dve = en(nc.semaphore("s_dve"))
        sem_gp = en(nc.semaphore("s_gp"))
        block = en(nc.Block())
        qall = qall_t.ap()
        wslot = wslot_t.ap()
        arena = arena_t.ap()
        G = G_t.ap()
        ONE = ONE_t.ap()
        AB = AB_t.ap()
        E = E_t.ap()
        ZP = ZP_t.ap()
        kvb = kvb_t.ap()
        KS = KS_t.ap()
        consts = consts_t.ap()
        carry = carry_t.ap()
        onescol = onescol_t.ap()
        pm = [p.ap() for p in (pm0, pm1, pm2, pm3, pm4)]
        pkv = pkv_t.ap()
        pax = [pax0.ap(), pax1.ap()]
        cbias = cbias_t.ap()
        for _i, _v in enumerate([EPS_ATTN, EPS_LN1, EPS_LN]):
            nc.const_aps.register(F32, _v, cbias[:, _i:_i + 1])

        def AV(off_bytes, cols, dt):
            width = cols * (2 if dt == BF16 else 4)
            return arena[:, off_bytes:off_bytes + width].bitcast(dt)

        xf2 = [AV(ch * 2048, T, F32) for ch in range(NCH)]
        ka = [AV(ch * 1024, T, BF16) for ch in range(NCH)]
        hs = [AV(16384 + ch * 2048, T, F32) for ch in range(NCH)]
        x1 = [AV(32768 + ch * 2048, T, F32) for ch in range(NCH)]
        mix = [AV(49152 + ch * 1024, T, BF16) for ch in range(NCH)]
        mstage = [AV(49152, 1024, F32), AV(53248, 1024, F32)]
        pool = [AV(57344 + i * 2048, T, F32) for i in range(2)]
        bb16 = [AV(61440 + i * 1024, T, BF16) for i in range(2)]
        xb = [[AV(p * 8192 + ch * 1024, T, BF16) for ch in range(NCH)] for p in range(2)]
        xfp = [AV(16384 + i * 2048, T, F32) for i in range(4)]
        ktm = [AV(24576, 1024, BF16), AV(26624, 1024, BF16),
               AV(53248, 1024, BF16)]
        vtm = [AV(28672, 1024, BF16), AV(30720, 1024, BF16),
               AV(55296, 1024, BF16)]
        bias_tm = AV(32768, 2048, F32)
        poolp1 = [AV(40960 + i * 2048, T, F32) for i in range(8)]
        kvacc = AV(57344, 512, F32)
        ksacc = AV(59392, 16, F32)

        a_slot = AB[0:1, 0:512]
        b_slot = AB[0:1, 512:1024]
        stA = AB[0:1, 512:1024]
        stB = pool[0].bitcast(F32)[0:1, 0:512]
        stC = AB[0:1, 0:512]
        zscr = AV(61440, 512, F32)[0:16, 0:512]
        z16 = ZP[0:16, 0:512]

        cn = ["bq", "bd", "mur", "muk", "omr", "omk", "beta"]
        ccol = {n: consts[:, 8 * i:8 * i + 8] for i, n in enumerate(cn)}

        B_q = [[Buf() for _ in range(NCH)] for _ in range(NTILES)]
        B_w = {n: [Buf() for _ in range(NCH)] for n in wnames}
        B_xf2 = [Buf() for _ in range(NCH)]
        B_hs = [Buf() for _ in range(NCH)]
        B_x1 = [Buf() for _ in range(NCH)]
        B_ka = [Buf() for _ in range(NCH)]
        B_mix = [Buf() for _ in range(NCH)]
        B_pool = [Buf() for _ in range(2)]
        B_bb16 = [Buf() for _ in range(2)]
        B_xb = [[Buf() for _ in range(NCH)] for _ in range(2)]
        B_xfp = [Buf() for _ in range(4)]
        B_ktm = [Buf() for _ in range(3)]
        B_vtm = [Buf() for _ in range(3)]

        B_biastm = Buf()
        B_poolp1 = [Buf() for _ in range(8)]
        B_poolp1[6] = B_ktm[2]
        B_poolp1[7] = B_vtm[2]
        B_pm = [Buf() for _ in range(5)]
        B_pkv = Buf()
        B_pax = [Buf() for _ in range(2)]
        B_ksum = [Buf() for _ in range(2)]
        B_kvb = [Buf() for _ in range(2)]
        B_KS = [Buf() for _ in range(2)]
        B_carry = Buf()
        B_G = Buf()
        B_ONE = Buf()
        B_E = Buf()
        B_ZP = Buf()
        B_a = Buf()
        B_b = Buf()
        B_stA = Buf()
        B_stB = Buf()
        B_stC = Buf()
        B_zscr = Buf()
        B_consts = Buf()
        B_onescol = Buf()
        B_cbias = Buf()
        B_out = [Buf() for _ in range(NCH)]

        pm_i = [0]
        pm_depth = [5]

        def next_pm():
            i = pm_i[0] % pm_depth[0]
            pm_i[0] += 1
            return i

        pax_i = [0]
        pax_depth = [2]

        def next_pax():
            if pax_depth[0] == 8:
                return next_pm()
            i = pax_i[0] % pax_depth[0]
            pax_i[0] += 1
            return i

        def ka_access(ch):
            return [B_ka[ch], B_xf2[ch // 2]]

        for i, n in enumerate(["bq", "bd", "mur", "muk"]):
            s.dma("sp", "dset", ccol[n], VEC[n].rearrange("(j p) -> p j", p=128),
                  writes=[B_consts], allow_slow_non_contiguous=True)
        s.dma("sp", "dset", ccol["beta"], VEC["beta"].rearrange("(j p) -> p j", p=128),
              writes=[B_consts], allow_slow_non_contiguous=True)
        s.dma("sp", "dset", E[:], econst, writes=[B_E])
        s.dma("sp", "dset", G[:], gconst, writes=[B_G])
        s.dma("sp", "dset", ONE[:], oneconst, writes=[B_ONE])
        s.dma("sp", "dset", bias_tm[:, 0:1024],
              VEC["bk"][None, :].partition_broadcast(128), writes=[B_biastm])
        s.dma("sp", "dset", bias_tm[:, 1024:2048],
              VEC["bv"][None, :].partition_broadcast(128), writes=[B_biastm])
        for _b in (B_consts, B_biastm, B_E, B_G, B_ONE):
            _b.w = ("dset", s.dcnt["dset"])
        s.op("dve", lambda e: e.tensor_scalar(out=ccol["omr"], in0=ccol["mur"],
             scalar1=-1.0, scalar2=1.0, op0=ALU.mult, op1=ALU.add),
             reads=[B_consts], writes=[B_consts])
        s.op("dve", lambda e: e.tensor_scalar(out=ccol["omk"], in0=ccol["muk"],
             scalar1=-1.0, scalar2=1.0, op0=ALU.mult, op1=ALU.add),
             reads=[B_consts], writes=[B_consts])
        s.op("dve", lambda e: e.memset(AB[:], 0.0),
             writes=[B_a, B_b, B_stA, B_stB, B_stC, B_zscr])
        s.op("dve", lambda e: e.memset(ZP[:], 0.0), writes=[B_ZP])
        s.op("dve", lambda e: e.memset(KS[:], 0.0), writes=[B_KS[0], B_KS[1]])
        s.op("dve", lambda e: e.memset(onescol[:, 0:1], 1.0), writes=[B_onescol])
        for _i, _v in enumerate([EPS_ATTN, EPS_LN1, EPS_LN]):
            s.op("dve", lambda e, _i=_i, _v=_v: e.memset(cbias[:, _i:_i + 1], _v),
                 writes=[B_cbias])

        SLOT = {"wq": 0, "wk": 1, "wv": 2, "wd": 3, "wr": 0, "wk2": 1, "wv2": 2}

        def wsl(n, ch):
            return wslot[:, (SLOT[n] * NCH + ch) * C:(SLOT[n] * NCH + ch + 1) * C]

        def load_weight(n, stage_aps, stage_buf_sets, qeng, sems):
            for ch in range(NCH):
                sa = stage_aps[ch % 2]
                sbs = stage_buf_sets[ch % 2]
                s.dma(qeng, sems[ch % 2], sa, W[n][128 * ch:128 * (ch + 1), :],
                      writes=sbs)
                s.op("dve", lambda e, sa=sa, n=n, ch=ch: e.tensor_scalar(
                    out=wsl(n, ch), in0=sa, scalar1=1.0, scalar2=None, op0=ALU.mult),
                    reads=sbs, writes=[B_w[n][ch]])

        wstage = [AV(40960, 1024, F32), AV(45056, 1024, F32)]
        for n in ["wq", "wk", "wv", "wd"]:
            load_weight(n, wstage,
                        [[B_poolp1[0], B_poolp1[1]], [B_poolp1[2], B_poolp1[3]]],
                        "sp", ("dw0", "dw1"))

        pending_kv = []

        def kv_products(sub_abs, p2, first_sub):
                skv = sub_abs % 2
                kvb_ps = pkv if skv == 0 else pax[1]
                kvb_buf = B_pkv if skv == 0 else B_pax[1]
                kscols = 0 if skv == 0 else 8
                for ch in range(NCH):
                    for hh in range(2):
                        lhs = ktm[p2][:, ch * 128 + hh * 64:ch * 128 + hh * 64 + 64]
                        rhs = vtm[p2][:, ch * 128 + hh * 64:ch * 128 + hh * 64 + 64]
                        dst = kvb_ps[hh * 64:(hh + 1) * 64, ch * 64:(ch + 1) * 64]
                        tp = (0, 64) if hh == 1 else None
                        s.op("te", lambda e, lhs=lhs, rhs=rhs, dst=dst, tp=tp:
                             e.matmul(dst, lhs, rhs, start=True, stop=True,
                                      tile_position=tp, skip_group_check=True),
                            reads=[B_ktm[p2], B_vtm[p2]], writes=[kvb_buf])
                    s.op("te", lambda e, ch=ch, p2=p2, kscols=kscols: e.matmul(
                        pax[0][:, kscols + ch:kscols + ch + 1],
                        ktm[p2][:, ch * 128:(ch + 1) * 128], onescol[:, 0:1],
                        start=(ch == 0), stop=(ch == NCH - 1),
                        skip_group_check=True),
                        reads=[B_ktm[p2], B_onescol], writes=[B_ksum[skv]])
                if first_sub:
                    s.op("dve", lambda e, kvb_ps=kvb_ps: e.tensor_copy(
                         kvacc, kvb_ps[:, 0:512]),
                         reads=[kvb_buf], writes=[B_pool[0]])
                    s.op("dve", lambda e, kscols=kscols: e.tensor_copy(
                         ksacc[:, 0:8], pax[0][:, kscols:kscols + 8]),
                         reads=[B_ksum[skv]], writes=[B_pool[1]])
                else:
                    s.op("dve", lambda e, kvb_ps=kvb_ps: e.tensor_tensor(
                         out=kvacc, in0=kvacc,
                         in1=kvb_ps[:, 0:512], op=ALU.add),
                         reads=[kvb_buf, B_pool[0]], writes=[B_pool[0]])
                    s.op("dve", lambda e, kscols=kscols: e.tensor_tensor(
                         out=ksacc[:, 0:8],
                         in0=ksacc[:, 0:8], in1=pax[0][:, kscols:kscols + 8],
                         op=ALU.add),
                         reads=[B_ksum[skv]], writes=[B_pool[1]])

        for i in range(NTILES):
            par = i % 2
            batch = i // (NTILES // 2)
            t0 = i * T
            first_of_batch = (i % (NTILES // 2) == 0)
            last_of_batch = (i % (NTILES // 2) == (NTILES // 2) - 1)
            for ch in range(NCH):
                sl = xfp[ch % 4]
                s.dma("sp", "dx%d" % (ch % 4), sl,
                      xT[128 * ch:128 * (ch + 1), t0:t0 + T],
                      writes=[B_xfp[ch % 4]])
                s.op("dve", lambda e, sl=sl, par=par, ch=ch: e.tensor_scalar(
                    out=xb[par][ch], in0=sl, scalar1=1.0, scalar2=None, op0=ALU.mult),
                    reads=[B_xfp[ch % 4]], writes=[B_xb[par][ch]])
            for dch in range(NCH):
                b = next_pm()
                for cch in range(NCH):
                    s.op("te", lambda e, b=b, dch=dch, cch=cch, par=par: e.matmul(
                        pm[b][:, 0:T], wsl("wq", cch)[:, 128 * dch:128 * (dch + 1)],
                        xb[par][cch], start=(cch == 0), stop=(cch == NCH - 1),
                        skip_group_check=True),
                        reads=[B_w["wq"][cch], B_xb[par][cch]], writes=[B_pm[b]])
                ie, ir = (dch % 3) * 2, (dch % 3) * 2 + 1
                pe, pr = poolp1[ie], poolp1[ir]
                bq_ap = ccol["bq"][:, dch:dch + 1]
                s.op("act", lambda e, b=b, pe=pe, bq_ap=bq_ap: e.activation(
                    out=pe, in_=pm[b][:, 0:T], func=AF.Exp, bias=bq_ap, scale=1.0),
                    reads=[B_pm[b], B_consts], writes=[B_poolp1[ie]])
                s.op("act", lambda e, b=b, pr=pr, bq_ap=bq_ap: e.activation(
                    out=pr, in_=pm[b][:, 0:T], func=AF.Relu, bias=bq_ap, scale=1.0),
                    reads=[B_pm[b], B_consts], writes=[B_poolp1[ir]])
                qdst = qall[:, dch * NT + t0:dch * NT + t0 + T]
                s.op("dve", lambda e, pe=pe, pr=pr, qdst=qdst: e.scalar_tensor_tensor(
                    out=qdst, in0=pe, scalar=1.0, in1=pr, op0=ALU.min, op1=ALU.add),
                    reads=[B_poolp1[ie], B_poolp1[ir]], writes=[B_q[i][dch]])
            for sub in range(T // 128):
                p2 = (i * (T // 128) + sub) % 3
                first_sub = first_of_batch and sub == 0
                last_sub = last_of_batch and sub == (T // 128) - 1
                for dh in range(2):
                    bk_ = next_pm()
                    bv_ = next_pm()
                    for cch in range(NCH):
                        lhs = xb[par][cch][:, 128 * sub:128 * (sub + 1)]
                        s.op("te", lambda e, bk_=bk_, lhs=lhs, cch=cch, dh=dh: e.matmul(
                            pm[bk_][:, 0:T], lhs,
                            wsl("wk", cch)[:, 512 * dh:512 * (dh + 1)],
                            start=(cch == 0), stop=(cch == NCH - 1),
                            skip_group_check=True),
                            reads=[B_xb[par][cch], B_w["wk"][cch]], writes=[B_pm[bk_]])
                    for cch in range(NCH):
                        lhs = xb[par][cch][:, 128 * sub:128 * (sub + 1)]
                        s.op("te", lambda e, bv_=bv_, lhs=lhs, cch=cch, dh=dh: e.matmul(
                            pm[bv_][:, 0:T], lhs,
                            wsl("wv", cch)[:, 512 * dh:512 * (dh + 1)],
                            start=(cch == 0), stop=(cch == NCH - 1),
                            skip_group_check=True),
                            reads=[B_xb[par][cch], B_w["wv"][cch]], writes=[B_pm[bv_]])
                    iu, ie, ir = 3 * dh, 3 * dh + 1, 3 * dh + 2
                    pu, pe, pr = poolp1[iu], poolp1[ie], poolp1[ir]
                    bsl = bias_tm[:, 512 * dh:512 * (dh + 1)]
                    s.op("dve", lambda e, bk_=bk_, pu=pu, bsl=bsl: e.tensor_tensor(
                        out=pu, in0=pm[bk_][:, 0:T], in1=bsl, op=ALU.add),
                        reads=[B_pm[bk_], B_biastm], writes=[B_poolp1[iu]])
                    s.op("act", lambda e, pu=pu, pe=pe: e.activation(
                        out=pe, in_=pu, func=AF.Exp),
                        reads=[B_poolp1[iu]], writes=[B_poolp1[ie]])
                    s.op("act", lambda e, pu=pu, pr=pr: e.activation(
                        out=pr, in_=pu, func=AF.Relu),
                        reads=[B_poolp1[iu]], writes=[B_poolp1[ir]])
                    kdst = ktm[p2][:, 512 * dh:512 * (dh + 1)]
                    s.op("dve", lambda e, pe=pe, pr=pr, kdst=kdst:
                         e.scalar_tensor_tensor(
                        out=kdst, in0=pe, scalar=1.0, in1=pr, op0=ALU.min,
                        op1=ALU.add),
                        reads=[B_poolp1[ie], B_poolp1[ir]], writes=[B_ktm[p2]])
                    vdst = vtm[p2][:, 512 * dh:512 * (dh + 1)]
                    bsv = bias_tm[:, 1024 + 512 * dh:1024 + 512 * (dh + 1)]
                    s.op("dve", lambda e, bv_=bv_, vdst=vdst, bsv=bsv: e.tensor_tensor(
                        out=vdst, in0=pm[bv_][:, 0:T], in1=bsv, op=ALU.add),
                        reads=[B_pm[bv_], B_biastm], writes=[B_vtm[p2]])
                pending_kv.append(
                    ((i * (T // 128) + sub), p2, first_sub))
                if len(pending_kv) > 1:
                    kv_products(*pending_kv.pop(0))
            if last_of_batch:
                while pending_kv:
                    kv_products(*pending_kv.pop(0))
                s.op("dve", lambda e, batch=batch: e.tensor_scalar(
                    out=kvb[:, batch * 512:(batch + 1) * 512], in0=kvacc,
                    scalar1=1.0, scalar2=None, op0=ALU.mult),
                    reads=[B_pool[0]], writes=[B_kvb[batch]])
                for ch in range(NCH):
                    for hh in range(2):
                        h = 2 * ch + hh
                        ksrc = ksacc[hh * 64:(hh + 1) * 64, ch:ch + 1]
                        dst = KS[hh * 64:(hh + 1) * 64,
                                 batch * 128 + ch * 16 + h:
                                 batch * 128 + ch * 16 + h + 1]
                        s.op("act", lambda e, ksrc=ksrc, dst=dst: e.activation(
                            out=dst, in_=ksrc, func=AF.Copy, bias=EPS_ATTN,
                            scale=1.0),
                            reads=[B_pool[1], B_cbias], writes=[B_KS[batch]])

        s.barrier()
        pax.append(pkv)
        B_pax.append(B_pkv)
        pax_depth[0] = 3

        mst_bufs = [[B_mix[0], B_mix[1], B_mix[2], B_mix[3]],
                    [B_mix[4], B_mix[5], B_mix[6], B_mix[7]]]

        def emit_morph():
            for n in ["wr", "wk2", "wv2"]:
                load_weight(n, mstage, mst_bufs, "sp", ("dm0", "dm1"))

        def ln_chain(bs, bq_, eps):
            s.op("act", lambda e, bs=bs: e.activation(out=stA, in_=pm[bs][0:1, 0:T],
                 func=AF.Square, scale=RC),
                 reads=[B_pm[bs]], writes=[B_b])
            s.op("dve", lambda e, bq_=bq_: e.scalar_tensor_tensor(
                 out=stB, in0=pm[bq_][0:1, 0:T],
                 scalar=RC, in1=stA, op0=ALU.mult, op1=ALU.subtract),
                 reads=[B_pm[bq_], B_b], writes=[B_pool[0]])
            s.op("act", lambda e, eps=eps: e.activation(out=stC, in_=stB,
                 func=AF.Ln, bias=eps, scale=1.0),
                 reads=[B_pool[0], B_cbias], writes=[B_a])
            s.op("act", lambda e: e.activation(out=a_slot, in_=stC,
                 func=AF.Exp, scale=-0.5),
                 reads=[B_a], writes=[B_a])
            s.op("dve", lambda e, bs=bs: e.scalar_tensor_tensor(
                 out=b_slot, in0=pm[bs][0:1, 0:T],
                 scalar=-RC, in1=a_slot, op0=ALU.mult, op1=ALU.mult),
                 reads=[B_pm[bs], B_a], writes=[B_b])

        def stat_sum_pre(a, abuf, ch):
            s.op("act", lambda e, a=a, o=mix[ch]: e.activation(
                out=o, in_=a, func=AF.Copy), reads=[abuf], writes=[B_mix[ch]])

        def ln_stats(src_aps, src_bufs, prestaged):
            bs = next_pm()
            bq_ = next_pm()
            if not prestaged:
                for ch in range(NCH):
                    stat_sum_pre(src_aps[ch], src_bufs[ch], ch)
            for ch in range(NCH):
                s.op("te", lambda e, bs=bs, cs=mix[ch], ch=ch: e.matmul(
                    pm[bs][0:1, 0:T], onescol[:, 0:1], cs,
                    start=(ch == 0), stop=(ch == NCH - 1), skip_group_check=True),
                    reads=[B_mix[ch], B_onescol], writes=[B_pm[bs]])
                s.op("act", lambda e, cq=mix[ch], a=src_aps[ch]: e.activation(
                    out=cq, in_=a, func=AF.Square),
                    reads=[src_bufs[ch], B_mix[ch]], writes=[B_mix[ch]])
            for ch in range(NCH):
                s.op("te", lambda e, bq_=bq_, cq=mix[ch], ch=ch: e.matmul(
                    pm[bq_][0:1, 0:T], onescol[:, 0:1], cq,
                    start=(ch == 0), stop=(ch == NCH - 1), skip_group_check=True),
                    reads=[B_mix[ch], B_onescol], writes=[B_pm[bq_]])
            return bs, bq_

        def xf2_loads(i):
            t0 = i * T
            for ch in range(NCH):
                wl = [B_xf2[ch]]
                if ch < 4:
                    wl += [B_ka[2 * ch], B_ka[2 * ch + 1]]
                s.dma("sp", "df%d" % ch, xf2[ch],
                      xT[128 * ch:128 * (ch + 1), t0:t0 + T], writes=wl)

        def attn_early(i):
            batch = i // (NTILES // 2)
            t0 = i * T
            bd_ = next_pax()
            for ch in range(NCH):
                lhs = KS[:, batch * 128 + ch * 16:batch * 128 + (ch + 1) * 16]
                rhs = qall[:, ch * NT + t0:ch * NT + t0 + T]
                s.op("te", lambda e, bd_=bd_, lhs=lhs, rhs=rhs, ch=ch: e.matmul(
                    pax[bd_][0:16, 0:T], lhs, rhs, start=(ch == 0),
                    stop=(ch == NCH - 1), skip_group_check=True),
                    reads=[B_KS[batch], B_q[i][ch]], writes=[B_pax[bd_]])
            s.op("act", lambda e, bd_=bd_: e.activation(out=z16,
                 in_=pax[bd_][0:16, 0:T], func=AF.Ln),
                 reads=[B_pax[bd_]], writes=[B_ZP])
            s.op("act", lambda e: e.activation(out=z16, in_=z16, func=AF.Exp,
                 scale=-1.0),
                 reads=[B_ZP], writes=[B_ZP])
            s.op("dve", lambda e, bd_=bd_: e.tensor_tensor(out=zscr, in0=z16,
                 in1=pax[bd_][0:16, 0:T], op=ALU.mult),
                 reads=[B_ZP, B_pax[bd_]], writes=[B_bb16[0], B_bb16[1]])
            s.op("dve", lambda e: e.tensor_scalar(out=zscr, in0=zscr, scalar1=-1.0,
                 scalar2=2.0, op0=ALU.mult, op1=ALU.add),
                 reads=[B_bb16[0]], writes=[B_bb16[0], B_bb16[1]])
            s.op("dve", lambda e: e.tensor_tensor(out=z16, in0=z16, in1=zscr,
                 op=ALU.mult),
                 reads=[B_bb16[0], B_ZP], writes=[B_ZP])
            zbanks = {}

            def emit_zb(ch):
                bz = next_pax()
                zbanks[ch] = bz
                s.op("te", lambda e, bz=bz, ch=ch: e.matmul(
                    pax[bz][:, 0:T], E[:, ch * 128:(ch + 1) * 128], ZP[:, 0:T],
                    start=True, stop=True, skip_group_check=True),
                    reads=[B_E, B_ZP], writes=[B_pax[bz]])

            emit_zb(0)
            emit_zb(1)
            for ch in range(NCH):
                if ch + 2 < NCH:
                    emit_zb(ch + 2)
                bz = zbanks[ch]
                qsl = qall[:, ch * NT + t0:ch * NT + t0 + T]
                s.op("dve", lambda e, bz=bz, ch=ch, qsl=qsl: e.tensor_tensor(
                    out=bb16[ch % 2], in0=qsl, in1=pax[bz][:, 0:T], op=ALU.mult),
                    reads=[B_q[i][ch], B_pax[bz]], writes=[B_bb16[ch % 2]])
                bc = next_pm()
                for hh in range(2):
                    lhs = kvb[hh * 64:(hh + 1) * 64,
                              batch * 512 + ch * 64:batch * 512 + (ch + 1) * 64]
                    rhs = bb16[ch % 2][hh * 64:(hh + 1) * 64, 0:T]
                    dst = pm[bc][hh * 64:(hh + 1) * 64, 0:T]
                    tp = (64, 64) if hh == 1 else None
                    s.op("te", lambda e, lhs=lhs, rhs=rhs, dst=dst, tp=tp: e.matmul(
                        dst, lhs, rhs, start=True, stop=True, tile_position=tp,
                        skip_group_check=True),
                        reads=[B_kvb[batch], B_bb16[ch % 2]], writes=[B_pm[bc]])
                s.op("act", lambda e, bc=bc, qsl=qsl: e.activation(
                    out=qsl, in_=pm[bc][:, 0:T], func=AF.Copy),
                    reads=[B_pm[bc]], writes=[B_q[i][ch]])

        wd_banks = {}

        def wd_mms(i, dchs):
            t0 = i * T
            for dch in dchs:
                b = next_pm()
                wd_banks[dch] = b
                for cch in range(NCH):
                    rhs = qall[:, cch * NT + t0:cch * NT + t0 + T]
                    s.op("te", lambda e, b=b, dch=dch, cch=cch, rhs=rhs: e.matmul(
                        pm[b][:, 0:T], wsl("wd", cch)[:, 128 * dch:128 * (dch + 1)],
                        rhs, start=(cch == 0), stop=(cch == NCH - 1),
                        skip_group_check=True),
                        reads=[B_w["wd"][cch], B_q[i][cch]], writes=[B_pm[b]])

        def wd_evac(i, dch):
            b = wd_banks[dch]
            s.op("dve", lambda e, b=b, dch=dch: e.scalar_tensor_tensor(
                out=hs[dch], in0=pm[b][:, 0:T], scalar=ccol["bd"][:, dch:dch + 1],
                in1=xf2[dch], op0=ALU.add, op1=ALU.add),
                reads=[B_pm[b], B_xf2[dch], B_consts], writes=[B_hs[dch]])
            stat_sum_pre(hs[dch], B_hs[dch], dch)

        def wd_block(i):
            for dch in range(NCH):
                wd_mms(i, [dch])
                wd_evac(i, dch)

        def ln1_block(i):
            bs, bq_ = ln_stats(hs, B_hs, True)
            ln_chain(bs, bq_, EPS_LN1)
            for ch in range(NCH):
                ba = next_pax()
                bb = next_pax()
                gl = G[:, ch * 128:(ch + 1) * 128]
                s.op("te", lambda e, ba=ba, gl=gl: e.matmul(
                    pax[ba][:, 0:T], gl, AB[:, 0:512], start=True, stop=True,
                    skip_group_check=True),
                    reads=[B_G, B_a], writes=[B_pax[ba]])
                s.op("te", lambda e, bb=bb, gl=gl: e.matmul(
                    pax[bb][:, 0:T], gl, AB[:, 512:1024], start=True, stop=True,
                    skip_group_check=True),
                    reads=[B_G, B_b], writes=[B_pax[bb]])
                s.op("dve", lambda e, ba=ba, ch=ch: e.tensor_tensor(
                    out=hs[ch], in0=hs[ch], in1=pax[ba][:, 0:T], op=ALU.mult),
                    reads=[B_hs[ch], B_pax[ba]], writes=[B_hs[ch]])
                s.op("dve", lambda e, bb=bb, ch=ch: e.scalar_tensor_tensor(
                    out=xf2[ch], in0=pax[bb][:, 0:T],
                    scalar=ccol["beta"][:, ch:ch + 1], in1=xf2[ch],
                    op0=ALU.add, op1=ALU.add),
                    reads=[B_pax[bb], B_xf2[ch], B_consts], writes=[B_xf2[ch]])
                s.op("gp", lambda e, ch=ch: e.tensor_tensor(
                    out=hs[ch], in0=hs[ch], in1=xf2[ch], op=ALU.add),
                    reads=[B_hs[ch], B_xf2[ch]], writes=[B_hs[ch]])
                stat_sum_pre(hs[ch], B_hs[ch], ch)

        def ln2_block(i):
            bs, bq_ = ln_stats(hs, B_hs, True)
            ln_chain(bs, bq_, EPS_LN)
            ba = next_pax()
            bb = next_pax()
            s.op("te", lambda e, ba=ba: e.matmul(pax[ba][:, 0:T], ONE[:, 0:128],
                 AB[:, 0:512], start=True, stop=True, skip_group_check=True),
                 reads=[B_ONE, B_a], writes=[B_pax[ba]])
            s.op("te", lambda e, bb=bb: e.matmul(pax[bb][:, 0:T], ONE[:, 0:128],
                 AB[:, 512:1024], start=True, stop=True, skip_group_check=True),
                 reads=[B_ONE, B_b], writes=[B_pax[bb]])
            for ch in range(NCH):
                s.op("dve", lambda e, ba=ba, ch=ch: e.tensor_tensor(
                    out=x1[ch], in0=hs[ch], in1=pax[ba][:, 0:T], op=ALU.mult),
                    reads=[B_hs[ch], B_pax[ba]], writes=[B_x1[ch]])
                s.op("dve", lambda e, bb=bb, ch=ch: e.tensor_tensor(
                    out=x1[ch], in0=x1[ch], in1=pax[bb][:, 0:T], op=ALU.add),
                    reads=[B_x1[ch], B_pax[bb]], writes=[B_x1[ch]])

        def mk_mix(mu_name, om_name):
            for ch in range(NCH):
                po = pool[ch % 2]
                om = ccol[om_name][:, ch:ch + 1]
                s.op("act", lambda e, ch=ch, po=po, om=om: e.activation(
                    out=po[:, 1:T], in_=x1[ch][:, 0:T - 1], func=AF.Copy,
                    scale=om),
                    reads=[B_x1[ch], B_consts], writes=[B_pool[ch % 2]])
                s.op("act", lambda e, ch=ch, po=po, om=om: e.activation(
                    out=po[:, 0:1], in_=carry[:, ch:ch + 1], func=AF.Copy,
                    scale=om),
                    reads=[B_carry, B_consts], writes=[B_pool[ch % 2]])
                mu = ccol[mu_name][:, ch:ch + 1]
                s.op("dve", lambda e, ch=ch, po=po, mu=mu: e.scalar_tensor_tensor(
                    out=mix[ch], in0=x1[ch], scalar=mu, in1=po,
                    op0=ALU.mult, op1=ALU.add),
                    reads=[B_x1[ch], B_pool[ch % 2], B_consts],
                    writes=[B_mix[ch]])

        def ffn1(i):
            if i % (NTILES // 2) == 0:
                s.op("dve", lambda e: e.memset(carry[:], 0.0), writes=[B_carry])
            mk_mix("muk", "omk")
            for dch in range(NCH):
                b = next_pm()
                for cch in range(NCH):
                    s.op("te", lambda e, b=b, dch=dch, cch=cch: e.matmul(
                        pm[b][:, 0:T], wsl("wk2", cch)[:, 128 * dch:128 * (dch + 1)],
                        mix[cch], start=(cch == 0), stop=(cch == NCH - 1),
                        skip_group_check=True),
                        reads=[B_w["wk2"][cch], B_mix[cch]], writes=[B_pm[b]])
                po = pool[dch % 2]
                s.op("dve", lambda e, b=b, po=po: e.tensor_scalar(
                    out=po, in0=pm[b][:, 0:T], scalar1=0.0, scalar2=None,
                    op0=ALU.max),
                    reads=[B_pm[b]], writes=[B_pool[dch % 2]])
                s.op("act", lambda e, po=po, dch=dch: e.activation(
                    out=ka[dch], in_=po, func=AF.Square),
                    reads=[B_pool[dch % 2]], writes=ka_access(dch))

        def ffn2(i):
            mk_mix("mur", "omr")
            for ch in range(NCH):
                s.op("dve", lambda e, ch=ch: e.tensor_copy(
                    carry[:, ch:ch + 1], x1[ch][:, T - 1:T]),
                    reads=[B_x1[ch]], writes=[B_carry])
            for dch in range(NCH):
                br_ = next_pm()
                for cch in range(NCH):
                    s.op("te", lambda e, br_=br_, dch=dch, cch=cch: e.matmul(
                        pm[br_][:, 0:T], wsl("wr", cch)[:, 128 * dch:128 * (dch + 1)],
                        mix[cch], start=(cch == 0), stop=(cch == NCH - 1),
                        skip_group_check=True),
                        reads=[B_w["wr"][cch], B_mix[cch]], writes=[B_pm[br_]])
                bv_ = next_pm()
                for cch in range(NCH):
                    s.op("te", lambda e, bv_=bv_, dch=dch, cch=cch: e.matmul(
                        pm[bv_][:, 0:T], wsl("wv2", cch)[:, 128 * dch:128 * (dch + 1)],
                        ka[cch], start=(cch == 0), stop=(cch == NCH - 1),
                        skip_group_check=True),
                        reads=[B_w["wv2"][cch], B_ka[cch], B_xf2[cch // 2]],
                        writes=[B_pm[bv_]])
                pth = pool[dch % 2]
                prr = pool[(dch + 1) % 2]
                s.op("act", lambda e, br_=br_, pth=pth: e.activation(
                    out=pth, in_=pm[br_][:, 0:T], func=AF.Tanh, scale=0.5),
                    reads=[B_pm[br_]], writes=[B_pool[dch % 2]])
                s.op("dve", lambda e, pth=pth, prr=prr: e.tensor_scalar(
                    out=prr, in0=pth, scalar1=0.5, scalar2=0.5, op0=ALU.mult,
                    op1=ALU.add),
                    reads=[B_pool[dch % 2]], writes=[B_pool[(dch + 1) % 2]])
                s.op("dve", lambda e, bv_=bv_, prr=prr, pth=pth: e.tensor_tensor(
                    out=pth, in0=pm[bv_][:, 0:T], in1=prr, op=ALU.mult),
                    reads=[B_pm[bv_], B_pool[(dch + 1) % 2]],
                    writes=[B_pool[dch % 2]])
                s.op("gp", lambda e, pth=pth, dch=dch: e.tensor_tensor(
                    out=hs[dch], in0=pth, in1=x1[dch], op=ALU.add),
                    reads=[B_pool[dch % 2], B_x1[dch]], writes=[B_hs[dch]])

        def ln3_out(i, overlap_next):
            t0 = i * T
            bs, bq_ = ln_stats(hs, B_hs, False)
            if overlap_next:
                wd_mms(i + 1, [0, 1, 2])
            ln_chain(bs, bq_, EPS_LN)
            ba = next_pax()
            bb = next_pax()
            s.op("te", lambda e, ba=ba: e.matmul(pax[ba][:, 0:T], ONE[:, 0:128],
                 AB[:, 0:512], start=True, stop=True, skip_group_check=True),
                 reads=[B_ONE, B_a], writes=[B_pax[ba]])
            s.op("te", lambda e, bb=bb: e.matmul(pax[bb][:, 0:T], ONE[:, 0:128],
                 AB[:, 512:1024], start=True, stop=True, skip_group_check=True),
                 reads=[B_ONE, B_b], writes=[B_pax[bb]])
            for ch in range(NCH):
                s.op("dve", lambda e, ba=ba, ch=ch: e.tensor_tensor(
                    out=x1[ch], in0=hs[ch], in1=pax[ba][:, 0:T], op=ALU.mult),
                    reads=[B_hs[ch], B_pax[ba]], writes=[B_x1[ch]])
                s.op("dve", lambda e, bb=bb, ch=ch: e.tensor_tensor(
                    out=x1[ch], in0=x1[ch], in1=pax[bb][:, 0:T], op=ALU.add),
                    reads=[B_x1[ch], B_pax[bb]], writes=[B_x1[ch]])
                s.dma("sp", "do%d" % ch, outT[128 * ch:128 * (ch + 1), t0:t0 + T],
                      x1[ch], reads=[B_x1[ch]], writes=[B_out[ch]])

        xf2_loads(0)
        emit_morph()
        attn_early(0)
        wd_block(0)
        for i in range(NTILES):
            ln1_block(i)
            if i + 1 < NTILES:
                attn_early(i + 1)
            ln2_block(i)
            ffn1(i)
            ffn2(i)
            if i + 1 < NTILES:
                xf2_loads(i + 1)
            ln3_out(i, i + 1 < NTILES)
            if i + 1 < NTILES:
                for dch in (0, 1, 2):
                    wd_evac(i + 1, dch)
                for dch in (3, 4, 5, 6, 7):
                    wd_mms(i + 1, [dch])
                    wd_evac(i + 1, dch)

        sems = {"te": sem_te, "act": sem_act, "dve": sem_dve, "gp": sem_gp}
        for dname in sorted(s.dcnt):
            sems[dname] = en(nc.semaphore(dname))
        out_finals = [("do%d" % ch, s.dcnt["do%d" % ch]) for ch in range(NCH)]

        def play(eng_name, eng):
            for (waits, fn, inc_sem, inc_amt) in s.prog[eng_name]:
                for (sname, v) in waits:
                    eng.wait_ge(sems[sname], v)
                fn(eng).then_inc(sems[inc_sem], inc_amt)

        @block.tensor
        def _(e):
            play("te", e)

        @block.scalar
        def _(e):
            play("act", e)

        @block.vector
        def _(e):
            play("dve", e)

        @block.sync
        def _(e):
            play("sp", e)

        @block.gpsimd
        def _(e):
            play("gp", e)
            for dname, v in out_finals:
                e.wait_ge(sems[dname], v)

    return nc


_NC_CACHE = []


def kernel(**inputs):
    x = np.asarray(inputs["L0_attn_input_tensor"], dtype=np.float32)
    B, S, H = x.shape
    n_cores = 8
    per = B // n_cores

    wmap = {"wq": "L0_attn_Wq", "wk": "L0_attn_Wk", "wv": "L0_attn_Wv",
            "wd": "L0_attn_Wd", "wr": "L0_ffn_W_r", "wk2": "L0_ffn_W_k",
            "wv2": "L0_ffn_W_v"}
    vmap = {"bq": "L0_attn_bq", "bk": "L0_attn_bk", "bv": "L0_attn_bv",
            "bd": "L0_attn_bd", "gamma": "L0_attn_gamma", "beta": "L0_attn_beta",
            "mur": "L0_ffn_mu_r", "muk": "L0_ffn_mu_k"}
    common = {}
    for k, src in wmap.items():
        common[k] = np.ascontiguousarray(np.asarray(inputs[src], dtype=np.float32).T)
    for k, src in vmap.items():
        common[k] = np.ascontiguousarray(np.asarray(inputs[src], dtype=np.float32))

    ecst = np.zeros((128, H), dtype=np.float32)
    for ch in range(8):
        ecst[2 * ch, ch * 128:ch * 128 + 64] = 1.0
        ecst[2 * ch + 1, ch * 128 + 64:ch * 128 + 128] = 1.0
    gcst = np.zeros((128, H), dtype=np.float32)
    gcst[0, :] = common["gamma"]
    ocst = np.zeros((128, 128), dtype=np.float32)
    ocst[0, :] = 1.0
    common["econst"] = ecst
    common["gconst"] = gcst
    common["oneconst"] = ocst

    in_maps = []
    for c in range(n_cores):
        shard = x[c * per:(c + 1) * per].reshape(per * S, H)
        m = dict(common)
        m["xT"] = np.ascontiguousarray(shard.T)
        in_maps.append(m)

    if not _NC_CACHE:
        _NC_CACHE.append(build_program())
    nc = _NC_CACHE[0]
    res = run_bass_kernel_spmd(nc, in_maps, core_ids=list(range(n_cores)))
    out = np.empty((B, S, H), dtype=np.float32)
    for c in range(n_cores):
        out[c * per:(c + 1) * per] = res.results[c]["outT"].T.reshape(per, S, H)
    return out


# version 16
# speedup vs baseline: 1.0769x; 1.0047x over previous
import math
import numpy as np
import concourse.bass as bass
import concourse.mybir as mybir
from concourse.bass_utils import run_bass_kernel_spmd

F32 = mybir.dt.float32
BF16 = mybir.dt.bfloat16
U8 = mybir.dt.uint8
AF = mybir.ActivationFunctionType
ALU = mybir.AluOpType

C = 1024
NCH = 8
T = 512
NT = 4096
NTILES = NT // T
NH = 16
HD = 64
EPS_ATTN = 1e-6
EPS_LN1 = 1e-12
EPS_LN = 1e-5
RC = 1.0 / C

ENGS = ("te", "act", "dve", "gp", "sp")
SELF_WAIT = {"act", "dve", "gp"}


class Buf:
    __slots__ = ("w", "r")

    def __init__(self):
        self.w = None
        self.r = []


class Sched:
    def __init__(self):
        self.cnt = {e: 0 for e in ENGS}
        self.dcnt = {}
        self.last_wait = {e: {} for e in ENGS}
        self.prog = {e: [] for e in ENGS}
        self.barrier_toks = []

    def _collect(self, eng, reads, writes, deps):
        need = list(deps) + list(self.barrier_toks)
        for b in reads:
            if b.w is not None:
                need.append(b.w)
        for b in writes:
            need.extend(b.r)
            if b.w is not None:
                need.append(b.w)
        waits = []
        for (sname, v) in need:
            if sname == eng and eng not in SELF_WAIT:
                continue
            lw = self.last_wait[eng].get(sname, 0)
            if v > lw:
                self.last_wait[eng][sname] = v
                waits.append((sname, v))
        return waits

    def op(self, eng, fn, reads=(), writes=(), deps=()):
        waits = self._collect(eng, reads, writes, deps)
        self.cnt[eng] += 1
        tok = (eng, self.cnt[eng])
        self.prog[eng].append((waits, fn, eng, 1))
        for b in reads:
            b.r.append(tok)
        for b in writes:
            b.w = tok
            b.r = []
        return tok

    def dma(self, qeng, dsem, out, in_, reads=(), writes=(), deps=(), **kw):
        sem = dsem
        waits = self._collect(qeng, reads, writes, deps)
        fn = lambda e, out=out, in_=in_, kw=kw: e.dma_start(out=out, in_=in_, **kw)
        self.prog[qeng].append((waits, fn, sem, 16))
        self.dcnt[sem] = self.dcnt.get(sem, 0) + 16
        tok = (sem, self.dcnt[sem])
        for b in reads:
            b.r.append(tok)
        for b in writes:
            b.w = tok
            b.r = []
        return tok

    def barrier(self):
        toks = [(e, self.cnt[e]) for e in ("te", "act", "dve", "gp") if self.cnt[e]]
        toks += [(sn, v) for sn, v in self.dcnt.items() if v]
        self.barrier_toks = toks


def build_program():
    nc = bass.Bass("TRN2", target_bir_lowering=False)

    xT = nc.dram_tensor("xT", [C, NT], F32, kind="ExternalInput").ap()
    outT = nc.dram_tensor("outT", [C, NT], F32, kind="ExternalOutput").ap()
    wnames = ["wq", "wk", "wv", "wd", "wr", "wk2", "wv2"]
    W = {n: nc.dram_tensor(n, [C, C], F32, kind="ExternalInput").ap() for n in wnames}
    econst = nc.dram_tensor("econst", [128, C], F32, kind="ExternalInput").ap()
    gconst = nc.dram_tensor("gconst", [128, C], F32, kind="ExternalInput").ap()
    oneconst = nc.dram_tensor("oneconst", [128, 128], F32, kind="ExternalInput").ap()
    vnames = ["bq", "bk", "bv", "bd", "gamma", "beta", "mur", "muk"]
    VEC = {n: nc.dram_tensor(n, [C], F32, kind="ExternalInput").ap() for n in vnames}

    s = Sched()

    from contextlib import ExitStack
    with ExitStack() as stack:
        en = stack.enter_context
        qall_t = en(nc.sbuf_tensor("qall", [128, NCH * NT], BF16))
        wslot_t = en(nc.sbuf_tensor("wslot", [128, 4 * NCH * C], BF16))
        arena_t = en(nc.sbuf_tensor("arena", [128, 63488], U8))
        G_t = en(nc.sbuf_tensor("G128", [128, 1024], F32))
        ONE_t = en(nc.sbuf_tensor("ONE128", [128, 128], F32))
        AB_t = en(nc.sbuf_tensor("AB", [128, 1024], F32))
        E_t = en(nc.sbuf_tensor("EALL", [128, 1024], F32))
        ZP_t = en(nc.sbuf_tensor("ZP", [128, 512], F32))
        kvb_t = en(nc.sbuf_tensor("kvb", [128, 1024], BF16))
        KS_t = en(nc.sbuf_tensor("KS", [128, 256], BF16))
        consts_t = en(nc.sbuf_tensor("consts", [128, 56], F32))
        carry_t = en(nc.sbuf_tensor("carry", [128, 8], F32))
        onescol_t = en(nc.sbuf_tensor("onescol", [128, 8], BF16))
        cbias_t = en(nc.sbuf_tensor("cbias", [128, 4], F32))
        pm0 = en(nc.psum_tensor("pm0", [128, 512], F32))
        pm1 = en(nc.psum_tensor("pm1", [128, 512], F32))
        pm2 = en(nc.psum_tensor("pm2", [128, 512], F32))
        pm3 = en(nc.psum_tensor("pm3", [128, 512], F32))
        pm4 = en(nc.psum_tensor("pm4", [128, 512], F32))
        pkv_t = en(nc.psum_tensor("pkv", [128, 512], F32))
        pax0 = en(nc.psum_tensor("pax0", [128, 512], F32))
        pax1 = en(nc.psum_tensor("pax1", [128, 512], F32))
        sem_te = en(nc.semaphore("s_te"))
        sem_act = en(nc.semaphore("s_act"))
        sem_dve = en(nc.semaphore("s_dve"))
        sem_gp = en(nc.semaphore("s_gp"))
        block = en(nc.Block())
        qall = qall_t.ap()
        wslot = wslot_t.ap()
        arena = arena_t.ap()
        G = G_t.ap()
        ONE = ONE_t.ap()
        AB = AB_t.ap()
        E = E_t.ap()
        ZP = ZP_t.ap()
        kvb = kvb_t.ap()
        KS = KS_t.ap()
        consts = consts_t.ap()
        carry = carry_t.ap()
        onescol = onescol_t.ap()
        pm = [p.ap() for p in (pm0, pm1, pm2, pm3, pm4)]
        pkv = pkv_t.ap()
        pax = [pax0.ap(), pax1.ap()]
        cbias = cbias_t.ap()
        for _i, _v in enumerate([EPS_ATTN, EPS_LN1, EPS_LN]):
            nc.const_aps.register(F32, _v, cbias[:, _i:_i + 1])

        def AV(off_bytes, cols, dt):
            width = cols * (2 if dt == BF16 else 4)
            return arena[:, off_bytes:off_bytes + width].bitcast(dt)

        xf2 = [AV(ch * 2048, T, F32) for ch in range(NCH)]
        ka = [AV(ch * 1024, T, BF16) for ch in range(NCH)]
        hs = [AV(16384 + ch * 2048, T, F32) for ch in range(NCH)]
        x1 = [AV(32768 + ch * 2048, T, F32) for ch in range(NCH)]
        mix = [AV(49152 + ch * 1024, T, BF16) for ch in range(NCH)]
        mstage = [AV(49152, 1024, F32), AV(53248, 1024, F32)]
        pool = [AV(57344 + i * 2048, T, F32) for i in range(2)]
        bb16 = [AV(61440 + i * 1024, T, BF16) for i in range(2)]
        xb = [[AV(p * 8192 + ch * 1024, T, BF16) for ch in range(NCH)] for p in range(2)]
        xfp = [AV(16384 + i * 2048, T, F32) for i in range(4)]
        ktm = [AV(24576, 1024, BF16), AV(26624, 1024, BF16),
               AV(53248, 1024, BF16)]
        vtm = [AV(28672, 1024, BF16), AV(30720, 1024, BF16),
               AV(55296, 1024, BF16)]
        bias_tm = AV(32768, 2048, F32)
        poolp1 = [AV(40960 + i * 2048, T, F32) for i in range(8)]
        kvacc = AV(57344, 512, F32)
        ksacc = AV(59392, 16, F32)

        a_slot = AB[0:1, 0:512]
        b_slot = AB[0:1, 512:1024]
        stA = AB[0:1, 512:1024]
        stB = pool[0].bitcast(F32)[0:1, 0:512]
        stC = AB[0:1, 0:512]
        zscr = AV(61440, 512, F32)[0:16, 0:512]
        z16 = ZP[0:16, 0:512]

        cn = ["bq", "bd", "mur", "muk", "omr", "omk", "beta"]
        ccol = {n: consts[:, 8 * i:8 * i + 8] for i, n in enumerate(cn)}

        B_q = [[Buf() for _ in range(NCH)] for _ in range(NTILES)]
        B_w = {n: [Buf() for _ in range(NCH)] for n in wnames}
        B_xf2 = [Buf() for _ in range(NCH)]
        B_hs = [Buf() for _ in range(NCH)]
        B_x1 = [Buf() for _ in range(NCH)]
        B_ka = [Buf() for _ in range(NCH)]
        B_mix = [Buf() for _ in range(NCH)]
        B_pool = [Buf() for _ in range(2)]
        B_bb16 = [Buf() for _ in range(2)]
        B_xb = [[Buf() for _ in range(NCH)] for _ in range(2)]
        B_xfp = [Buf() for _ in range(4)]
        B_ktm = [Buf() for _ in range(3)]
        B_vtm = [Buf() for _ in range(3)]

        B_biastm = Buf()
        B_poolp1 = [Buf() for _ in range(8)]
        B_poolp1[6] = B_ktm[2]
        B_poolp1[7] = B_vtm[2]
        B_pm = [Buf() for _ in range(5)]
        B_pkv = Buf()
        B_pax = [Buf() for _ in range(2)]
        B_ksum = [Buf() for _ in range(2)]
        B_kvb = [Buf() for _ in range(2)]
        B_KS = [Buf() for _ in range(2)]
        B_carry = Buf()
        B_G = Buf()
        B_ONE = Buf()
        B_E = Buf()
        B_ZP = Buf()
        B_a = Buf()
        B_b = Buf()
        B_stA = Buf()
        B_stB = Buf()
        B_stC = Buf()
        B_zscr = Buf()
        B_consts = Buf()
        B_onescol = Buf()
        B_cbias = Buf()
        B_out = [Buf() for _ in range(NCH)]

        pm_i = [0]
        pm_depth = [5]

        def next_pm():
            i = pm_i[0] % pm_depth[0]
            pm_i[0] += 1
            return i

        pax_i = [0]
        pax_depth = [2]

        def next_pax():
            if pax_depth[0] == 8:
                return next_pm()
            i = pax_i[0] % pax_depth[0]
            pax_i[0] += 1
            return i

        def ka_access(ch):
            return [B_ka[ch], B_xf2[ch // 2]]

        for i, n in enumerate(["bq", "bd", "mur", "muk"]):
            s.dma("sp", "dset", ccol[n], VEC[n].rearrange("(j p) -> p j", p=128),
                  writes=[B_consts], allow_slow_non_contiguous=True)
        s.dma("sp", "dset", ccol["beta"], VEC["beta"].rearrange("(j p) -> p j", p=128),
              writes=[B_consts], allow_slow_non_contiguous=True)
        s.dma("sp", "dset", E[:], econst, writes=[B_E])
        s.dma("sp", "dset", G[:], gconst, writes=[B_G])
        s.dma("sp", "dset", ONE[:], oneconst, writes=[B_ONE])
        s.dma("sp", "dset", bias_tm[:, 0:1024],
              VEC["bk"][None, :].partition_broadcast(128), writes=[B_biastm])
        s.dma("sp", "dset", bias_tm[:, 1024:2048],
              VEC["bv"][None, :].partition_broadcast(128), writes=[B_biastm])
        for _b in (B_consts, B_biastm, B_E, B_G, B_ONE):
            _b.w = ("dset", s.dcnt["dset"])
        s.op("dve", lambda e: e.tensor_scalar(out=ccol["omr"], in0=ccol["mur"],
             scalar1=-1.0, scalar2=1.0, op0=ALU.mult, op1=ALU.add),
             reads=[B_consts], writes=[B_consts])
        s.op("dve", lambda e: e.tensor_scalar(out=ccol["omk"], in0=ccol["muk"],
             scalar1=-1.0, scalar2=1.0, op0=ALU.mult, op1=ALU.add),
             reads=[B_consts], writes=[B_consts])
        s.op("dve", lambda e: e.memset(AB[:], 0.0),
             writes=[B_a, B_b, B_stA, B_stB, B_stC, B_zscr])
        s.op("dve", lambda e: e.memset(ZP[:], 0.0), writes=[B_ZP])
        s.op("dve", lambda e: e.memset(KS[:], 0.0), writes=[B_KS[0], B_KS[1]])
        s.op("dve", lambda e: e.memset(onescol[:, 0:1], 1.0), writes=[B_onescol])
        for _i, _v in enumerate([EPS_ATTN, EPS_LN1, EPS_LN]):
            s.op("dve", lambda e, _i=_i, _v=_v: e.memset(cbias[:, _i:_i + 1], _v),
                 writes=[B_cbias])

        SLOT = {"wq": 0, "wk": 1, "wv": 2, "wd": 3, "wr": 0, "wk2": 1, "wv2": 2}

        def wsl(n, ch):
            return wslot[:, (SLOT[n] * NCH + ch) * C:(SLOT[n] * NCH + ch + 1) * C]

        def load_weight(n, stage_aps, stage_buf_sets, qeng, sems):
            for ch in range(NCH):
                sa = stage_aps[ch % 2]
                sbs = stage_buf_sets[ch % 2]
                s.dma(qeng, sems[ch % 2], sa, W[n][128 * ch:128 * (ch + 1), :],
                      writes=sbs)
                s.op("dve", lambda e, sa=sa, n=n, ch=ch: e.tensor_scalar(
                    out=wsl(n, ch), in0=sa, scalar1=1.0, scalar2=None, op0=ALU.mult),
                    reads=sbs, writes=[B_w[n][ch]])

        wstage = [AV(40960, 1024, F32), AV(45056, 1024, F32)]
        for n in ["wq", "wk", "wv", "wd"]:
            load_weight(n, wstage,
                        [[B_poolp1[0], B_poolp1[1]], [B_poolp1[2], B_poolp1[3]]],
                        "sp", ("dw0", "dw1"))

        pending_kv = []

        def kv_products(sub_abs, p2, first_sub):
                skv = sub_abs % 2
                kvb_ps = pkv if skv == 0 else pax[1]
                kvb_buf = B_pkv if skv == 0 else B_pax[1]
                kscols = 0 if skv == 0 else 8
                for ch in range(NCH):
                    for hh in range(2):
                        lhs = ktm[p2][:, ch * 128 + hh * 64:ch * 128 + hh * 64 + 64]
                        rhs = vtm[p2][:, ch * 128 + hh * 64:ch * 128 + hh * 64 + 64]
                        dst = kvb_ps[hh * 64:(hh + 1) * 64, ch * 64:(ch + 1) * 64]
                        tp = (0, 64) if hh == 1 else None
                        s.op("te", lambda e, lhs=lhs, rhs=rhs, dst=dst, tp=tp:
                             e.matmul(dst, lhs, rhs, start=True, stop=True,
                                      tile_position=tp, skip_group_check=True),
                            reads=[B_ktm[p2], B_vtm[p2]], writes=[kvb_buf])
                    s.op("te", lambda e, ch=ch, p2=p2, kscols=kscols: e.matmul(
                        pax[0][:, kscols + ch:kscols + ch + 1],
                        ktm[p2][:, ch * 128:(ch + 1) * 128], onescol[:, 0:1],
                        start=(ch == 0), stop=(ch == NCH - 1),
                        skip_group_check=True),
                        reads=[B_ktm[p2], B_onescol], writes=[B_ksum[skv]])
                if first_sub:
                    s.op("dve", lambda e, kvb_ps=kvb_ps: e.tensor_copy(
                         kvacc, kvb_ps[:, 0:512]),
                         reads=[kvb_buf], writes=[B_pool[0]])
                    s.op("dve", lambda e, kscols=kscols: e.tensor_copy(
                         ksacc[:, 0:8], pax[0][:, kscols:kscols + 8]),
                         reads=[B_ksum[skv]], writes=[B_pool[1]])
                else:
                    s.op("dve", lambda e, kvb_ps=kvb_ps: e.tensor_tensor(
                         out=kvacc, in0=kvacc,
                         in1=kvb_ps[:, 0:512], op=ALU.add),
                         reads=[kvb_buf, B_pool[0]], writes=[B_pool[0]])
                    s.op("dve", lambda e, kscols=kscols: e.tensor_tensor(
                         out=ksacc[:, 0:8],
                         in0=ksacc[:, 0:8], in1=pax[0][:, kscols:kscols + 8],
                         op=ALU.add),
                         reads=[B_ksum[skv]], writes=[B_pool[1]])

        for i in range(NTILES):
            par = i % 2
            batch = i // (NTILES // 2)
            t0 = i * T
            first_of_batch = (i % (NTILES // 2) == 0)
            last_of_batch = (i % (NTILES // 2) == (NTILES // 2) - 1)
            for ch in range(NCH):
                sl = xfp[ch % 4]
                s.dma("sp", "dx%d" % (ch % 4), sl,
                      xT[128 * ch:128 * (ch + 1), t0:t0 + T],
                      writes=[B_xfp[ch % 4]])
                s.op("dve", lambda e, sl=sl, par=par, ch=ch: e.tensor_scalar(
                    out=xb[par][ch], in0=sl, scalar1=1.0, scalar2=None, op0=ALU.mult),
                    reads=[B_xfp[ch % 4]], writes=[B_xb[par][ch]])
            for dch in range(NCH):
                b = next_pm()
                for cch in range(NCH):
                    s.op("te", lambda e, b=b, dch=dch, cch=cch, par=par: e.matmul(
                        pm[b][:, 0:T], wsl("wq", cch)[:, 128 * dch:128 * (dch + 1)],
                        xb[par][cch], start=(cch == 0), stop=(cch == NCH - 1),
                        skip_group_check=True),
                        reads=[B_w["wq"][cch], B_xb[par][cch]], writes=[B_pm[b]])
                ie, ir = (dch % 3) * 2, (dch % 3) * 2 + 1
                pe, pr = poolp1[ie], poolp1[ir]
                bq_ap = ccol["bq"][:, dch:dch + 1]
                s.op("act", lambda e, b=b, pe=pe, bq_ap=bq_ap: e.activation(
                    out=pe, in_=pm[b][:, 0:T], func=AF.Exp, bias=bq_ap, scale=1.0),
                    reads=[B_pm[b], B_consts], writes=[B_poolp1[ie]])
                s.op("act", lambda e, b=b, pr=pr, bq_ap=bq_ap: e.activation(
                    out=pr, in_=pm[b][:, 0:T], func=AF.Relu, bias=bq_ap, scale=1.0),
                    reads=[B_pm[b], B_consts], writes=[B_poolp1[ir]])
                qdst = qall[:, dch * NT + t0:dch * NT + t0 + T]
                s.op("dve", lambda e, pe=pe, pr=pr, qdst=qdst: e.scalar_tensor_tensor(
                    out=qdst, in0=pe, scalar=1.0, in1=pr, op0=ALU.min, op1=ALU.add),
                    reads=[B_poolp1[ie], B_poolp1[ir]], writes=[B_q[i][dch]])
            for sub in range(T // 128):
                p2 = (i * (T // 128) + sub) % 3
                first_sub = first_of_batch and sub == 0
                last_sub = last_of_batch and sub == (T // 128) - 1
                for dh in range(2):
                    bk_ = next_pm()
                    bv_ = next_pm()
                    for cch in range(NCH):
                        lhs = xb[par][cch][:, 128 * sub:128 * (sub + 1)]
                        s.op("te", lambda e, bk_=bk_, lhs=lhs, cch=cch, dh=dh: e.matmul(
                            pm[bk_][:, 0:T], lhs,
                            wsl("wk", cch)[:, 512 * dh:512 * (dh + 1)],
                            start=(cch == 0), stop=(cch == NCH - 1),
                            skip_group_check=True),
                            reads=[B_xb[par][cch], B_w["wk"][cch]], writes=[B_pm[bk_]])
                    for cch in range(NCH):
                        lhs = xb[par][cch][:, 128 * sub:128 * (sub + 1)]
                        s.op("te", lambda e, bv_=bv_, lhs=lhs, cch=cch, dh=dh: e.matmul(
                            pm[bv_][:, 0:T], lhs,
                            wsl("wv", cch)[:, 512 * dh:512 * (dh + 1)],
                            start=(cch == 0), stop=(cch == NCH - 1),
                            skip_group_check=True),
                            reads=[B_xb[par][cch], B_w["wv"][cch]], writes=[B_pm[bv_]])
                    iu, ie, ir = 3 * dh, 3 * dh + 1, 3 * dh + 2
                    pu, pe, pr = poolp1[iu], poolp1[ie], poolp1[ir]
                    bsl = bias_tm[:, 512 * dh:512 * (dh + 1)]
                    s.op("dve", lambda e, bk_=bk_, pu=pu, bsl=bsl: e.tensor_tensor(
                        out=pu, in0=pm[bk_][:, 0:T], in1=bsl, op=ALU.add),
                        reads=[B_pm[bk_], B_biastm], writes=[B_poolp1[iu]])
                    s.op("act", lambda e, pu=pu, pe=pe: e.activation(
                        out=pe, in_=pu, func=AF.Exp),
                        reads=[B_poolp1[iu]], writes=[B_poolp1[ie]])
                    s.op("act", lambda e, pu=pu, pr=pr: e.activation(
                        out=pr, in_=pu, func=AF.Relu),
                        reads=[B_poolp1[iu]], writes=[B_poolp1[ir]])
                    kdst = ktm[p2][:, 512 * dh:512 * (dh + 1)]
                    s.op("dve", lambda e, pe=pe, pr=pr, kdst=kdst:
                         e.scalar_tensor_tensor(
                        out=kdst, in0=pe, scalar=1.0, in1=pr, op0=ALU.min,
                        op1=ALU.add),
                        reads=[B_poolp1[ie], B_poolp1[ir]], writes=[B_ktm[p2]])
                    vdst = vtm[p2][:, 512 * dh:512 * (dh + 1)]
                    bsv = bias_tm[:, 1024 + 512 * dh:1024 + 512 * (dh + 1)]
                    s.op("dve", lambda e, bv_=bv_, vdst=vdst, bsv=bsv: e.tensor_tensor(
                        out=vdst, in0=pm[bv_][:, 0:T], in1=bsv, op=ALU.add),
                        reads=[B_pm[bv_], B_biastm], writes=[B_vtm[p2]])
                pending_kv.append(
                    ((i * (T // 128) + sub), p2, first_sub))
                if len(pending_kv) > 2:
                    kv_products(*pending_kv.pop(0))
            if last_of_batch:
                while pending_kv:
                    kv_products(*pending_kv.pop(0))
                s.op("dve", lambda e, batch=batch: e.tensor_scalar(
                    out=kvb[:, batch * 512:(batch + 1) * 512], in0=kvacc,
                    scalar1=1.0, scalar2=None, op0=ALU.mult),
                    reads=[B_pool[0]], writes=[B_kvb[batch]])
                for ch in range(NCH):
                    for hh in range(2):
                        h = 2 * ch + hh
                        ksrc = ksacc[hh * 64:(hh + 1) * 64, ch:ch + 1]
                        dst = KS[hh * 64:(hh + 1) * 64,
                                 batch * 128 + ch * 16 + h:
                                 batch * 128 + ch * 16 + h + 1]
                        s.op("act", lambda e, ksrc=ksrc, dst=dst: e.activation(
                            out=dst, in_=ksrc, func=AF.Copy, bias=EPS_ATTN,
                            scale=1.0),
                            reads=[B_pool[1], B_cbias], writes=[B_KS[batch]])

        s.barrier()
        pax.append(pkv)
        B_pax.append(B_pkv)
        pax_depth[0] = 3

        mst_bufs = [[B_mix[0], B_mix[1], B_mix[2], B_mix[3]],
                    [B_mix[4], B_mix[5], B_mix[6], B_mix[7]]]

        def emit_morph():
            for n in ["wr", "wk2", "wv2"]:
                load_weight(n, mstage, mst_bufs, "sp", ("dm0", "dm1"))

        def ln_chain(bs, bq_, eps):
            s.op("act", lambda e, bs=bs: e.activation(out=stA, in_=pm[bs][0:1, 0:T],
                 func=AF.Square, scale=RC),
                 reads=[B_pm[bs]], writes=[B_b])
            s.op("dve", lambda e, bq_=bq_: e.scalar_tensor_tensor(
                 out=stB, in0=pm[bq_][0:1, 0:T],
                 scalar=RC, in1=stA, op0=ALU.mult, op1=ALU.subtract),
                 reads=[B_pm[bq_], B_b], writes=[B_pool[0]])
            s.op("act", lambda e, eps=eps: e.activation(out=stC, in_=stB,
                 func=AF.Ln, bias=eps, scale=1.0),
                 reads=[B_pool[0], B_cbias], writes=[B_a])
            s.op("act", lambda e: e.activation(out=a_slot, in_=stC,
                 func=AF.Exp, scale=-0.5),
                 reads=[B_a], writes=[B_a])
            s.op("dve", lambda e, bs=bs: e.scalar_tensor_tensor(
                 out=b_slot, in0=pm[bs][0:1, 0:T],
                 scalar=-RC, in1=a_slot, op0=ALU.mult, op1=ALU.mult),
                 reads=[B_pm[bs], B_a], writes=[B_b])

        def stat_sum_pre(a, abuf, ch):
            s.op("act", lambda e, a=a, o=mix[ch]: e.activation(
                out=o, in_=a, func=AF.Copy), reads=[abuf], writes=[B_mix[ch]])

        def ln_stats(src_aps, src_bufs, prestaged):
            bs = next_pm()
            bq_ = next_pm()
            if not prestaged:
                for ch in range(NCH):
                    stat_sum_pre(src_aps[ch], src_bufs[ch], ch)
            for ch in range(NCH):
                s.op("te", lambda e, bs=bs, cs=mix[ch], ch=ch: e.matmul(
                    pm[bs][0:1, 0:T], onescol[:, 0:1], cs,
                    start=(ch == 0), stop=(ch == NCH - 1), skip_group_check=True),
                    reads=[B_mix[ch], B_onescol], writes=[B_pm[bs]])
                s.op("act", lambda e, cq=mix[ch], a=src_aps[ch]: e.activation(
                    out=cq, in_=a, func=AF.Square),
                    reads=[src_bufs[ch], B_mix[ch]], writes=[B_mix[ch]])
            for ch in range(NCH):
                s.op("te", lambda e, bq_=bq_, cq=mix[ch], ch=ch: e.matmul(
                    pm[bq_][0:1, 0:T], onescol[:, 0:1], cq,
                    start=(ch == 0), stop=(ch == NCH - 1), skip_group_check=True),
                    reads=[B_mix[ch], B_onescol], writes=[B_pm[bq_]])
            return bs, bq_

        def xf2_loads(i):
            t0 = i * T
            for ch in range(NCH):
                wl = [B_xf2[ch]]
                if ch < 4:
                    wl += [B_ka[2 * ch], B_ka[2 * ch + 1]]
                s.dma("sp", "df%d" % ch, xf2[ch],
                      xT[128 * ch:128 * (ch + 1), t0:t0 + T], writes=wl)

        def attn_early(i):
            batch = i // (NTILES // 2)
            t0 = i * T
            bd_ = next_pax()
            for ch in range(NCH):
                lhs = KS[:, batch * 128 + ch * 16:batch * 128 + (ch + 1) * 16]
                rhs = qall[:, ch * NT + t0:ch * NT + t0 + T]
                s.op("te", lambda e, bd_=bd_, lhs=lhs, rhs=rhs, ch=ch: e.matmul(
                    pax[bd_][0:16, 0:T], lhs, rhs, start=(ch == 0),
                    stop=(ch == NCH - 1), skip_group_check=True),
                    reads=[B_KS[batch], B_q[i][ch]], writes=[B_pax[bd_]])
            s.op("act", lambda e, bd_=bd_: e.activation(out=z16,
                 in_=pax[bd_][0:16, 0:T], func=AF.Ln),
                 reads=[B_pax[bd_]], writes=[B_ZP])
            s.op("act", lambda e: e.activation(out=z16, in_=z16, func=AF.Exp,
                 scale=-1.0),
                 reads=[B_ZP], writes=[B_ZP])
            s.op("dve", lambda e, bd_=bd_: e.tensor_tensor(out=zscr, in0=z16,
                 in1=pax[bd_][0:16, 0:T], op=ALU.mult),
                 reads=[B_ZP, B_pax[bd_]], writes=[B_bb16[0], B_bb16[1]])
            s.op("dve", lambda e: e.tensor_scalar(out=zscr, in0=zscr, scalar1=-1.0,
                 scalar2=2.0, op0=ALU.mult, op1=ALU.add),
                 reads=[B_bb16[0]], writes=[B_bb16[0], B_bb16[1]])
            s.op("dve", lambda e: e.tensor_tensor(out=z16, in0=z16, in1=zscr,
                 op=ALU.mult),
                 reads=[B_bb16[0], B_ZP], writes=[B_ZP])
            zbanks = {}

            def emit_zb(ch):
                bz = next_pax()
                zbanks[ch] = bz
                s.op("te", lambda e, bz=bz, ch=ch: e.matmul(
                    pax[bz][:, 0:T], E[:, ch * 128:(ch + 1) * 128], ZP[:, 0:T],
                    start=True, stop=True, skip_group_check=True),
                    reads=[B_E, B_ZP], writes=[B_pax[bz]])

            emit_zb(0)
            emit_zb(1)
            for ch in range(NCH):
                if ch + 2 < NCH:
                    emit_zb(ch + 2)
                bz = zbanks[ch]
                qsl = qall[:, ch * NT + t0:ch * NT + t0 + T]
                s.op("dve", lambda e, bz=bz, ch=ch, qsl=qsl: e.tensor_tensor(
                    out=bb16[ch % 2], in0=qsl, in1=pax[bz][:, 0:T], op=ALU.mult),
                    reads=[B_q[i][ch], B_pax[bz]], writes=[B_bb16[ch % 2]])
                bc = next_pm()
                for hh in range(2):
                    lhs = kvb[hh * 64:(hh + 1) * 64,
                              batch * 512 + ch * 64:batch * 512 + (ch + 1) * 64]
                    rhs = bb16[ch % 2][hh * 64:(hh + 1) * 64, 0:T]
                    dst = pm[bc][hh * 64:(hh + 1) * 64, 0:T]
                    tp = (64, 64) if hh == 1 else None
                    s.op("te", lambda e, lhs=lhs, rhs=rhs, dst=dst, tp=tp: e.matmul(
                        dst, lhs, rhs, start=True, stop=True, tile_position=tp,
                        skip_group_check=True),
                        reads=[B_kvb[batch], B_bb16[ch % 2]], writes=[B_pm[bc]])
                s.op("act", lambda e, bc=bc, qsl=qsl: e.activation(
                    out=qsl, in_=pm[bc][:, 0:T], func=AF.Copy),
                    reads=[B_pm[bc]], writes=[B_q[i][ch]])

        wd_banks = {}

        def wd_mms(i, dchs):
            t0 = i * T
            for dch in dchs:
                b = next_pm()
                wd_banks[dch] = b
                for cch in range(NCH):
                    rhs = qall[:, cch * NT + t0:cch * NT + t0 + T]
                    s.op("te", lambda e, b=b, dch=dch, cch=cch, rhs=rhs: e.matmul(
                        pm[b][:, 0:T], wsl("wd", cch)[:, 128 * dch:128 * (dch + 1)],
                        rhs, start=(cch == 0), stop=(cch == NCH - 1),
                        skip_group_check=True),
                        reads=[B_w["wd"][cch], B_q[i][cch]], writes=[B_pm[b]])

        def wd_evac(i, dch):
            b = wd_banks[dch]
            s.op("dve", lambda e, b=b, dch=dch: e.scalar_tensor_tensor(
                out=hs[dch], in0=pm[b][:, 0:T], scalar=ccol["bd"][:, dch:dch + 1],
                in1=xf2[dch], op0=ALU.add, op1=ALU.add),
                reads=[B_pm[b], B_xf2[dch], B_consts], writes=[B_hs[dch]])
            stat_sum_pre(hs[dch], B_hs[dch], dch)

        def wd_block(i):
            for dch in range(NCH):
                wd_mms(i, [dch])
                wd_evac(i, dch)

        def ln1_block(i):
            bs, bq_ = ln_stats(hs, B_hs, True)
            ln_chain(bs, bq_, EPS_LN1)
            for ch in range(NCH):
                ba = next_pax()
                bb = next_pax()
                gl = G[:, ch * 128:(ch + 1) * 128]
                s.op("te", lambda e, ba=ba, gl=gl: e.matmul(
                    pax[ba][:, 0:T], gl, AB[:, 0:512], start=True, stop=True,
                    skip_group_check=True),
                    reads=[B_G, B_a], writes=[B_pax[ba]])
                s.op("te", lambda e, bb=bb, gl=gl: e.matmul(
                    pax[bb][:, 0:T], gl, AB[:, 512:1024], start=True, stop=True,
                    skip_group_check=True),
                    reads=[B_G, B_b], writes=[B_pax[bb]])
                s.op("dve", lambda e, ba=ba, ch=ch: e.tensor_tensor(
                    out=hs[ch], in0=hs[ch], in1=pax[ba][:, 0:T], op=ALU.mult),
                    reads=[B_hs[ch], B_pax[ba]], writes=[B_hs[ch]])
                s.op("dve", lambda e, bb=bb, ch=ch: e.scalar_tensor_tensor(
                    out=xf2[ch], in0=pax[bb][:, 0:T],
                    scalar=ccol["beta"][:, ch:ch + 1], in1=xf2[ch],
                    op0=ALU.add, op1=ALU.add),
                    reads=[B_pax[bb], B_xf2[ch], B_consts], writes=[B_xf2[ch]])
                s.op("gp", lambda e, ch=ch: e.tensor_tensor(
                    out=hs[ch], in0=hs[ch], in1=xf2[ch], op=ALU.add),
                    reads=[B_hs[ch], B_xf2[ch]], writes=[B_hs[ch]])
                stat_sum_pre(hs[ch], B_hs[ch], ch)

        def ln2_block(i):
            bs, bq_ = ln_stats(hs, B_hs, True)
            ln_chain(bs, bq_, EPS_LN)
            ba = next_pax()
            bb = next_pax()
            s.op("te", lambda e, ba=ba: e.matmul(pax[ba][:, 0:T], ONE[:, 0:128],
                 AB[:, 0:512], start=True, stop=True, skip_group_check=True),
                 reads=[B_ONE, B_a], writes=[B_pax[ba]])
            s.op("te", lambda e, bb=bb: e.matmul(pax[bb][:, 0:T], ONE[:, 0:128],
                 AB[:, 512:1024], start=True, stop=True, skip_group_check=True),
                 reads=[B_ONE, B_b], writes=[B_pax[bb]])
            for ch in range(NCH):
                s.op("dve", lambda e, ba=ba, ch=ch: e.tensor_tensor(
                    out=x1[ch], in0=hs[ch], in1=pax[ba][:, 0:T], op=ALU.mult),
                    reads=[B_hs[ch], B_pax[ba]], writes=[B_x1[ch]])
                s.op("dve", lambda e, bb=bb, ch=ch: e.tensor_tensor(
                    out=x1[ch], in0=x1[ch], in1=pax[bb][:, 0:T], op=ALU.add),
                    reads=[B_x1[ch], B_pax[bb]], writes=[B_x1[ch]])

        def mk_mix(mu_name, om_name):
            for ch in range(NCH):
                po = pool[ch % 2]
                om = ccol[om_name][:, ch:ch + 1]
                s.op("act", lambda e, ch=ch, po=po, om=om: e.activation(
                    out=po[:, 1:T], in_=x1[ch][:, 0:T - 1], func=AF.Copy,
                    scale=om),
                    reads=[B_x1[ch], B_consts], writes=[B_pool[ch % 2]])
                s.op("act", lambda e, ch=ch, po=po, om=om: e.activation(
                    out=po[:, 0:1], in_=carry[:, ch:ch + 1], func=AF.Copy,
                    scale=om),
                    reads=[B_carry, B_consts], writes=[B_pool[ch % 2]])
                mu = ccol[mu_name][:, ch:ch + 1]
                s.op("dve", lambda e, ch=ch, po=po, mu=mu: e.scalar_tensor_tensor(
                    out=mix[ch], in0=x1[ch], scalar=mu, in1=po,
                    op0=ALU.mult, op1=ALU.add),
                    reads=[B_x1[ch], B_pool[ch % 2], B_consts],
                    writes=[B_mix[ch]])

        def ffn1(i):
            if i % (NTILES // 2) == 0:
                s.op("dve", lambda e: e.memset(carry[:], 0.0), writes=[B_carry])
            mk_mix("muk", "omk")
            for dch in range(NCH):
                b = next_pm()
                for cch in range(NCH):
                    s.op("te", lambda e, b=b, dch=dch, cch=cch: e.matmul(
                        pm[b][:, 0:T], wsl("wk2", cch)[:, 128 * dch:128 * (dch + 1)],
                        mix[cch], start=(cch == 0), stop=(cch == NCH - 1),
                        skip_group_check=True),
                        reads=[B_w["wk2"][cch], B_mix[cch]], writes=[B_pm[b]])
                po = pool[dch % 2]
                s.op("dve", lambda e, b=b, po=po: e.tensor_scalar(
                    out=po, in0=pm[b][:, 0:T], scalar1=0.0, scalar2=None,
                    op0=ALU.max),
                    reads=[B_pm[b]], writes=[B_pool[dch % 2]])
                s.op("act", lambda e, po=po, dch=dch: e.activation(
                    out=ka[dch], in_=po, func=AF.Square),
                    reads=[B_pool[dch % 2]], writes=ka_access(dch))

        def ffn2(i):
            mk_mix("mur", "omr")
            for ch in range(NCH):
                s.op("dve", lambda e, ch=ch: e.tensor_copy(
                    carry[:, ch:ch + 1], x1[ch][:, T - 1:T]),
                    reads=[B_x1[ch]], writes=[B_carry])
            for dch in range(NCH):
                br_ = next_pm()
                for cch in range(NCH):
                    s.op("te", lambda e, br_=br_, dch=dch, cch=cch: e.matmul(
                        pm[br_][:, 0:T], wsl("wr", cch)[:, 128 * dch:128 * (dch + 1)],
                        mix[cch], start=(cch == 0), stop=(cch == NCH - 1),
                        skip_group_check=True),
                        reads=[B_w["wr"][cch], B_mix[cch]], writes=[B_pm[br_]])
                bv_ = next_pm()
                for cch in range(NCH):
                    s.op("te", lambda e, bv_=bv_, dch=dch, cch=cch: e.matmul(
                        pm[bv_][:, 0:T], wsl("wv2", cch)[:, 128 * dch:128 * (dch + 1)],
                        ka[cch], start=(cch == 0), stop=(cch == NCH - 1),
                        skip_group_check=True),
                        reads=[B_w["wv2"][cch], B_ka[cch], B_xf2[cch // 2]],
                        writes=[B_pm[bv_]])
                pth = pool[dch % 2]
                prr = pool[(dch + 1) % 2]
                s.op("act", lambda e, br_=br_, pth=pth: e.activation(
                    out=pth, in_=pm[br_][:, 0:T], func=AF.Tanh, scale=0.5),
                    reads=[B_pm[br_]], writes=[B_pool[dch % 2]])
                s.op("dve", lambda e, pth=pth, prr=prr: e.tensor_scalar(
                    out=prr, in0=pth, scalar1=0.5, scalar2=0.5, op0=ALU.mult,
                    op1=ALU.add),
                    reads=[B_pool[dch % 2]], writes=[B_pool[(dch + 1) % 2]])
                s.op("dve", lambda e, bv_=bv_, prr=prr, pth=pth: e.tensor_tensor(
                    out=pth, in0=pm[bv_][:, 0:T], in1=prr, op=ALU.mult),
                    reads=[B_pm[bv_], B_pool[(dch + 1) % 2]],
                    writes=[B_pool[dch % 2]])
                s.op("gp", lambda e, pth=pth, dch=dch: e.tensor_tensor(
                    out=hs[dch], in0=pth, in1=x1[dch], op=ALU.add),
                    reads=[B_pool[dch % 2], B_x1[dch]], writes=[B_hs[dch]])

        def ln3_out(i, overlap_next):
            t0 = i * T
            bs, bq_ = ln_stats(hs, B_hs, False)
            if overlap_next:
                wd_mms(i + 1, [0, 1, 2])
            ln_chain(bs, bq_, EPS_LN)
            ba = next_pax()
            bb = next_pax()
            s.op("te", lambda e, ba=ba: e.matmul(pax[ba][:, 0:T], ONE[:, 0:128],
                 AB[:, 0:512], start=True, stop=True, skip_group_check=True),
                 reads=[B_ONE, B_a], writes=[B_pax[ba]])
            s.op("te", lambda e, bb=bb: e.matmul(pax[bb][:, 0:T], ONE[:, 0:128],
                 AB[:, 512:1024], start=True, stop=True, skip_group_check=True),
                 reads=[B_ONE, B_b], writes=[B_pax[bb]])
            for ch in range(NCH):
                s.op("dve", lambda e, ba=ba, ch=ch: e.tensor_tensor(
                    out=x1[ch], in0=hs[ch], in1=pax[ba][:, 0:T], op=ALU.mult),
                    reads=[B_hs[ch], B_pax[ba]], writes=[B_x1[ch]])
                s.op("dve", lambda e, bb=bb, ch=ch: e.tensor_tensor(
                    out=x1[ch], in0=x1[ch], in1=pax[bb][:, 0:T], op=ALU.add),
                    reads=[B_x1[ch], B_pax[bb]], writes=[B_x1[ch]])
                s.dma("sp", "do%d" % ch, outT[128 * ch:128 * (ch + 1), t0:t0 + T],
                      x1[ch], reads=[B_x1[ch]], writes=[B_out[ch]])

        xf2_loads(0)
        emit_morph()
        attn_early(0)
        wd_block(0)
        for i in range(NTILES):
            ln1_block(i)
            if i + 1 < NTILES:
                attn_early(i + 1)
            ln2_block(i)
            ffn1(i)
            ffn2(i)
            if i + 1 < NTILES:
                xf2_loads(i + 1)
            ln3_out(i, i + 1 < NTILES)
            if i + 1 < NTILES:
                for dch in (0, 1, 2):
                    wd_evac(i + 1, dch)
                for dch in (3, 4, 5, 6, 7):
                    wd_mms(i + 1, [dch])
                    wd_evac(i + 1, dch)

        sems = {"te": sem_te, "act": sem_act, "dve": sem_dve, "gp": sem_gp}
        for dname in sorted(s.dcnt):
            sems[dname] = en(nc.semaphore(dname))
        out_finals = [("do%d" % ch, s.dcnt["do%d" % ch]) for ch in range(NCH)]

        def play(eng_name, eng):
            for (waits, fn, inc_sem, inc_amt) in s.prog[eng_name]:
                for (sname, v) in waits:
                    eng.wait_ge(sems[sname], v)
                fn(eng).then_inc(sems[inc_sem], inc_amt)

        @block.tensor
        def _(e):
            play("te", e)

        @block.scalar
        def _(e):
            play("act", e)

        @block.vector
        def _(e):
            play("dve", e)

        @block.sync
        def _(e):
            play("sp", e)

        @block.gpsimd
        def _(e):
            play("gp", e)
            for dname, v in out_finals:
                e.wait_ge(sems[dname], v)

    return nc


_NC_CACHE = []


def kernel(**inputs):
    x = np.asarray(inputs["L0_attn_input_tensor"], dtype=np.float32)
    B, S, H = x.shape
    n_cores = 8
    per = B // n_cores

    wmap = {"wq": "L0_attn_Wq", "wk": "L0_attn_Wk", "wv": "L0_attn_Wv",
            "wd": "L0_attn_Wd", "wr": "L0_ffn_W_r", "wk2": "L0_ffn_W_k",
            "wv2": "L0_ffn_W_v"}
    vmap = {"bq": "L0_attn_bq", "bk": "L0_attn_bk", "bv": "L0_attn_bv",
            "bd": "L0_attn_bd", "gamma": "L0_attn_gamma", "beta": "L0_attn_beta",
            "mur": "L0_ffn_mu_r", "muk": "L0_ffn_mu_k"}
    common = {}
    for k, src in wmap.items():
        common[k] = np.ascontiguousarray(np.asarray(inputs[src], dtype=np.float32).T)
    for k, src in vmap.items():
        common[k] = np.ascontiguousarray(np.asarray(inputs[src], dtype=np.float32))

    ecst = np.zeros((128, H), dtype=np.float32)
    for ch in range(8):
        ecst[2 * ch, ch * 128:ch * 128 + 64] = 1.0
        ecst[2 * ch + 1, ch * 128 + 64:ch * 128 + 128] = 1.0
    gcst = np.zeros((128, H), dtype=np.float32)
    gcst[0, :] = common["gamma"]
    ocst = np.zeros((128, 128), dtype=np.float32)
    ocst[0, :] = 1.0
    common["econst"] = ecst
    common["gconst"] = gcst
    common["oneconst"] = ocst

    in_maps = []
    for c in range(n_cores):
        shard = x[c * per:(c + 1) * per].reshape(per * S, H)
        m = dict(common)
        m["xT"] = np.ascontiguousarray(shard.T)
        in_maps.append(m)

    if not _NC_CACHE:
        _NC_CACHE.append(build_program())
    nc = _NC_CACHE[0]
    res = run_bass_kernel_spmd(nc, in_maps, core_ids=list(range(n_cores)))
    out = np.empty((B, S, H), dtype=np.float32)
    for c in range(n_cores):
        out[c * per:(c + 1) * per] = res.results[c]["outT"].T.reshape(per, S, H)
    return out
